# version 14
# baseline (speedup 1.0000x reference)
import numpy as np
import ml_dtypes

import concourse.bass as bass
import concourse.mybir as mybir
from concourse.bass_utils import run_bass_kernel_spmd
from contextlib import ExitStack

F32 = mybir.dt.float32
BF16 = mybir.dt.bfloat16
F16 = mybir.dt.float16
AX = mybir.AxisListType.X
OP = mybir.AluOpType
AF = mybir.ActivationFunctionType
bfdt = ml_dtypes.bfloat16

B_, T, D = 4, 4096, 1024
HD = 64
NHC = 8
NT = T // 128
NCH = D // 128
BIGNEG = -800.0
EPS_LN = 1e-5


NSP = T // 512
PHASES = 3
CSTEP = 99
PIPE_ATTN = True


def set_T(t):
    global T, NT, NSP
    T = t
    NT = T // 128
    NSP = T // 512


def _consts():
    ident = np.eye(128, dtype=bfdt)
    ident32 = np.eye(128, dtype=np.float32)
    A = (np.arange(128)[:, None] <= np.arange(128)[None, :]).astype(bfdt)
    p = np.arange(128)[:, None]
    q = np.arange(896)[None, :]
    Bwide = (BIGNEG * (p > (q - 384))).astype(bfdt)
    h = np.array([[1.0]], dtype=np.float64)
    while h.shape[0] < D:
        h = np.block([[h, h], [h, -h]])
    H = (h / 32.0).astype(bfdt)
    ones_row = np.ones((1, 128), dtype=np.float32)
    selAB = np.zeros((128, 2), dtype=bfdt)
    selAB[:64, 0] = 1
    selAB[64:, 1] = 1
    return ident, ident32, A, Bwide, H, ones_row, selAB


class Sem:
    __slots__ = ("h", "c")

    def __init__(self, h):
        self.h = h
        self.c = 0


def build_nc(debug=False):
    nc = bass.Bass("TRN2", target_bir_lowering=False, detect_race_conditions=False)

    xb = nc.dram_tensor("xb", [T, D], F32, kind="ExternalInput").ap()
    x_ocol = nc.dram_tensor("x_ocol", [T, 512], F32, kind="ExternalInput").ap()
    wq_d = nc.dram_tensor("wq", [512, D], F32, kind="ExternalInput").ap()
    wk_d = nc.dram_tensor("wk", [512, D], F32, kind="ExternalInput").ap()
    wv_d = nc.dram_tensor("wv", [512, D], F32, kind="ExternalInput").ap()
    wo_d = nc.dram_tensor("wo", [512, D], F32, kind="ExternalInput").ap()
    wf_d = nc.dram_tensor("wf", [512, D], F32, kind="ExternalInput").ap()
    bq_d = nc.dram_tensor("bq", [512], F32, kind="ExternalInput").ap()
    bk_d = nc.dram_tensor("bk", [512], F32, kind="ExternalInput").ap()
    bv_d = nc.dram_tensor("bv", [512], F32, kind="ExternalInput").ap()
    bo_d = nc.dram_tensor("bo", [512], F32, kind="ExternalInput").ap()
    out_d = nc.dram_tensor("out", [T, 512], F32, kind="ExternalOutput").ap()

    dbg = {}
    if debug:
        for nm, sh in (("xn", [T, D]), ("xq8", [NCH, 128, T]), ("vaug", [NHC, 128, NT * 129]),
                       ("y", [NHC, 128, NT * HD]), ("xq4", [NCH, 128, T]), ("yq", [T, 512]),
                       ("sca", [1, 16]), ("yqf", [128, NCH * T]), ("woT", [128, 4096]),
                       ("lnst", [128, 11 * NT]),
                       ("wfT", [128, 4096]), ("wsrep", [128, 512]), ("rs4", [128, NT])):
            dbg[nm] = nc.dram_tensor("dbg_" + nm, sh, F32, kind="ExternalOutput").ap()

    xn_sp = nc.dram_tensor("xn_sp", [T, D], F32)
    xq8_sp = nc.dram_tensor("xq8_sp", [NCH, 128, T], BF16)
    xq4_sp = nc.dram_tensor("xq4_sp", [NCH, 128, T], BF16)
    vaug_sp = nc.dram_tensor("vaug_sp", [NHC, 128, NT * 129], BF16)
    y_sp = nc.dram_tensor("y_sp", [NHC, 128, NT * HD], F32)
    cc1_in = nc.dram_tensor("cc1_in", [5], F32)
    cc1_out = nc.dram_tensor("cc1_out", [5], F32)
    cc2_in = nc.dram_tensor("cc2_in", [1], F32)
    cc2_out = nc.dram_tensor("cc2_out", [1], F32)
    wsb = nc.dram_tensor("wsb", [512], F32)
    ag_in = nc.dram_tensor("ag_in", [T, 512], BF16)
    ag_out = nc.dram_tensor("ag_out", [2, T, 512], BF16)

    ident_np, ident32_np, A_np, Bw_np, H_np, ones_np, selAB_np = _consts()
    ident_d = nc.inline_tensor(ident_np, "c_ident").ap()
    ident32_d = nc.inline_tensor(ident32_np, "c_ident32").ap()
    A_d = nc.inline_tensor(A_np, "c_A").ap()
    Bw_d = nc.inline_tensor(Bw_np, "c_Bw").ap()
    H_d = nc.inline_tensor(H_np, "c_H").ap()
    ones_d = nc.inline_tensor(ones_np, "c_ones").ap()
    selAB_d = nc.inline_tensor(selAB_np, "c_selAB").ap()

    stx = ExitStack()
    sbuf = lambda n_, s, d: stx.enter_context(nc.sbuf_tensor(n_, s, d))

    big1 = sbuf("big1", [128, 32832], BF16)
    big2 = sbuf("big2", [128, 32768], BF16)
    wt = sbuf("wt", [128, 4 * 4096 + 8192], BF16)
    xst = sbuf("xst", [128, 2 * D], F32)
    xnt = sbuf("xnt", [128, 2 * D], F32)
    q16 = sbuf("q16", [128, D], F16)
    qtm = sbuf("qtm", [128, D], BF16)
    vstage = sbuf("vstage", [128, 2 * 8 * 129], BF16)
    stats = sbuf("stats", [128, 32], F32)
    lnst = sbuf("lnst", [128, 11 * NT], F32)
    scal = sbuf("scal", [128, 8], F32)
    sbc = sbuf("sbc", [128, 12], F32)
    srow = sbuf("srow", [1, 16], F32)
    mrow = sbuf("mrow", [1, 16], F32)
    mbc = sbuf("mbc", [128, 8], F32)
    cpool = sbuf("cpool", [128, 8], F32)
    consts = sbuf("consts", [128, 128 + 128 + 896], BF16)
    ident32 = sbuf("ident32", [128, 128], F32)
    onesr = sbuf("onesr", [1, 128], F32)
    selAB = sbuf("selAB", [128, 2], BF16)
    brow = sbuf("brow", [128, 8], F32)
    bvrep = sbuf("bvrep", [128, 512], F32)
    wfs = sbuf("wfs", [128, 4], F32)
    rs4 = sbuf("rs4", [128, NT], F32)
    tmp1 = xst[:, 0:512]
    tmp2 = xst[:, 512:1024]
    qn2 = sbuf("qn2", [128, 6], F32)

    xq8f = [big1[:, ch * T:(ch + 1) * T] for ch in range(NCH)]
    stKs = [big1[:, 0 * T:1 * T], big1[:, 3 * T:4 * T]]
    stQHs = [big1[:, 1 * T:2 * T], big1[:, 4 * T:5 * T]]
    stQLs = [big1[:, 2 * T:3 * T], big1[:, 5 * T:6 * T]]
    vaug0 = big1[:, 6 * T:6 * T + NT * 129]
    yh0 = 6 * T + NT * 129 + (NT * 129) % 2
    yhead = big1[:, yh0:yh0 + NT * 128].bitcast(F32)
    yqf = [big1[:, ch * T:(ch + 1) * T] for ch in range(NCH)]

    wraw = [big2[:, i * 2048:(i + 1) * 2048].bitcast(F32) for i in range(16)]
    pairQH = big2[:, 0 * T:1 * T]
    pairQL = big2[:, 1 * T:2 * T]
    pairKH = big2[:, 2 * T:3 * T]
    pairKL = big2[:, 3 * T:4 * T]
    xq8stg = [big2[:, 4 * T + i * T:4 * T + (i + 1) * T] for i in range(2)]
    RINGN = 4
    r0 = 6 * T
    ringPh = [big2[:, r0 + i * 512:r0 + (i + 1) * 512] for i in range(RINGN)]
    r1 = r0 + RINGN * 512
    ringPl = [big2[:, r1 + i * 512:r1 + (i + 1) * 512] for i in range(RINGN)]
    r2 = r1 + RINGN * 512
    pf32 = [big2[:, r2:r2 + 2048].bitcast(F32),
            big2[:, r2 + 2048:r2 + 4096].bitcast(F32)]
    sqq = big2[:, 4 * T:5 * T]
    vaug1 = big2[:, 0:NT * 129]
    vaugs = [vaug0, vaug1]
    ha = 4 * T if T < 4096 else 0
    hi_tm = big2[:, ha + 0:ha + 1024]
    lo_tm = big2[:, ha + 1024:ha + 2048]
    hif = big2[:, ha + 2048:ha + 3072]
    lof = big2[:, ha + 3072:ha + 4096]
    yq_tm = big2
    wfraw = [big2[:, NT * 512 + i * 2048:NT * 512 + (i + 1) * 2048].bitcast(F32) for i in range(4)]
    xq4stg = [big2[:, NT * 512 + i * 1024:NT * 512 + (i + 1) * 1024] for i in range(2)]
    ystg = big2[:, NT * 512 + 8192:NT * 512 + 8192 + 4096].bitcast(F32)
    aq_stg = [big2[:, NT * 512 + 12288 + i * 512:NT * 512 + 12288 + (i + 1) * 512] for i in range(2)]

    wqT = wt[:, 0 * 4096:1 * 4096]
    wkT = wt[:, 1 * 4096:2 * 4096]
    wvT = wt[:, 2 * 4096:3 * 4096]
    woT = wt[:, 3 * 4096:4 * 4096]
    hbuf = wt[:, 4 * 4096:4 * 4096 + 8192]
    wfT = wt[:, 0 * 4096:1 * 4096]
    wsrep = wt[:, 4 * 4096:4 * 4096 + 1024].bitcast(F32)
    borep = wt[:, 4 * 4096 + 1024:4 * 4096 + 2048].bitcast(F32)
    xot = [wt[:, 4 * 4096 + 2048 + i * 1024:4 * 4096 + 2048 + (i + 1) * 1024].bitcast(F32)
           for i in range(2)]
    obuf = [wt[:, 4 * 4096 + 4096 + i * 1024:4 * 4096 + 4096 + (i + 1) * 1024].bitcast(F32)
            for i in range(2)]

    identv = consts[:, 0:128]
    Av = consts[:, 128:256]
    Bv = [consts[:, 256 + 384 - 128 * c: 256 + 896 - 128 * c] for c in range(4)]

    MC_XN, MC_WQ, MC_WK, MC_WV, MC_WO, MC_Y = 0, 1, 2, 3, 4, 5
    SC_RX, SC_AQ, SC_AK, SC_AV, SC_RWQ, SC_RWK, SC_RWV, SC_RWO, SC_RY, SC_AO = range(10)
    CP_125, CP_1536, CP_N1536, CP_1 = range(4)

    pmm = stx.enter_context(nc.psum_tensor("pmm", [128, 8, 512], F32))
    sc_b = [pmm[:, 0, :], pmm[:, 1, :]]
    pvb = [pmm[:, 2 + i, :] for i in range(4)]
    mm_b = [pmm[:, 6, :], pmm[:, 7, :]]
    tp0 = pmm[:, 2, :].bitcast(BF16)
    tp1 = pmm[:, 3, :].bitcast(BF16)
    tp2 = pmm[:, 4, :].bitcast(BF16)
    tp32 = pmm[:, 5, :]
    tpB = pmm[:, 7, :]

    sems = []

    def mksem(name):
        s = Sem(stx.enter_context(nc.semaphore(name)))
        sems.append(s)
        return s

    sy = mksem("sy")
    sg = mksem("sg")
    sq = mksem("sq")
    sv = mksem("sv")
    ss = mksem("ss")
    s2 = mksem("s2")
    scc = mksem("scc")
    sd = mksem("sd")

    snap = {id(s): 0 for s in sems}

    def mkwalk(cur, eng):
        for s in sems:
            s.c = snap[id(s)]

        def mk(name):
            def em(fn, inc=None, n=1, dup=False):
                reps = 2 if dup else 1
                for _ in range(reps):
                    if cur == name:
                        ins = fn(eng)
                        if inc is not None:
                            ins.then_inc(inc.h, n)
                    if inc is not None:
                        inc.c += n

            def wt_(sem, val=None):
                if cur == name:
                    eng.wait_ge(sem.h, sem.c if val is None else val)
            return em, wt_
        return mk

    def phaseA(cur, eng):
        mk = mkwalk(cur, eng)
        VE, VW = mk("vector")
        TE, TW = mk("tensor")
        SE, SW = mk("scalar")
        YE, YW = mk("sync")
        GE, GW = mk("gpsimd")

        YE(lambda e: e.dma_start(out=identv, in_=ident_d), sy, 16)
        YE(lambda e: e.dma_start(out=Av, in_=A_d), sy, 16)
        YE(lambda e: e.dma_start(out=consts[:, 256:256 + 896], in_=Bw_d), sy, 16)
        YE(lambda e: e.dma_start(out=ident32[:, :], in_=ident32_d), sy, 16)
        YE(lambda e: e.dma_start(out=onesr[:, :], in_=ones_d), sy, 16)
        YE(lambda e: e.dma_start(out=selAB[:, :], in_=selAB_d), sy, 16)
        YE(lambda e: e.dma_start(out=hbuf.rearrange("p (c j) -> p c j", j=1024),
                                 in_=H_d.rearrange("(c p) j -> p c j", p=128)), sy, 16)
        cst_y = sy.c

        for bd, col in ((bq_d, 0), (bk_d, 4)):
            GE(lambda e, bd=bd, col=col: e.dma_start(
                out=brow[:, col:col + 4], in_=bd.rearrange("(c p) -> p c", p=128),
                allow_slow_non_contiguous=True), sg, 16)
        for i, wd in enumerate((wq_d, wk_d, wv_d, wo_d)):
            for ch in range(4):
                GE(lambda e, wd=wd, k=i * 4 + ch, ch=ch: e.dma_start(
                    out=wraw[k], in_=wd[ch * 128:(ch + 1) * 128, :]), sg, 16)
        wld_g = sg.c

        for col, val in ((CP_125, 0.125), (CP_1536, 1536.0), (CP_N1536, -1536.0), (CP_1, 1.0)):
            VE(lambda e, col=col, val=val: e.memset(cpool[:, col:col + 1], val), sv, 1)
        VE(lambda e: e.memset(srow[0:1, :], 0.0), sv, 1, dup=True)
        VW(sv, sv.c)
        VE(lambda e: e.memset(mrow[0:1, :], 0.0), sv, 1, dup=True)
        VW(sv, sv.c)

        LS_SUM, LS_MAX, LS_MIN, LS_S1, LS_S2, LS_RS, LS_NM, LS_XM = (i * NT for i in range(8))
        p1_free = [None] * NT
        p1_free_s = [None] * NT
        for j in range(NT):
            xt = xst[:, (j % 2) * D:((j % 2) + 1) * D]
            if j >= 2:
                YW(sv, p1_free[j - 2])
                YW(ss, p1_free_s[j - 2])
            YE(lambda e, j=j, xt=xt: e.dma_start(out=xt, in_=xb[j * 128:(j + 1) * 128, :]), sy, 16)
            VW(sy, cst_y + 16 * (j + 1))
            VE(lambda e, xt=xt, j=j: e.tensor_reduce(lnst[:, LS_SUM + j:LS_SUM + j + 1], xt, AX, OP.add), sv, 1, dup=True)
            VE(lambda e, xt=xt, j=j: e.tensor_reduce(lnst[:, LS_MAX + j:LS_MAX + j + 1], xt, AX, OP.max), sv, 1, dup=True)
            VE(lambda e, xt=xt, j=j: e.tensor_reduce(lnst[:, LS_MIN + j:LS_MIN + j + 1], xt, AX, OP.min), sv, 1, dup=True)
            SW(sy, cst_y + 16 * (j + 1))
            if j >= 1:
                SW(sv, psred[j - 1])
            SE(lambda e, xt=xt: e.activation(mm_b[0], xt[:, 0:512], AF.Square, bias=0.0, scale=1.0), ss, 1, dup=True)
            SE(lambda e, xt=xt: e.activation(mm_b[1], xt[:, 512:1024], AF.Square, bias=0.0, scale=1.0), ss, 1, dup=True)
            p1_free_s[j] = ss.c
            VW(ss, ss.c)
            VE(lambda e, j=j: e.tensor_reduce(lnst[:, LS_S1 + j:LS_S1 + j + 1], mm_b[0], AX, OP.add), sv, 1, dup=True)
            VE(lambda e, j=j: e.tensor_reduce(lnst[:, LS_S2 + j:LS_S2 + j + 1], mm_b[1], AX, OP.add), sv, 1, dup=True)
            if j == 0:
                psred = [None] * NT
            psred[j] = sv.c
            p1_free[j] = sv.c

        LS_T1, LS_T2, LS_T3 = 8 * NT, 9 * NT, 10 * NT
        L = lambda o: lnst[:, o:o + NT]
        VE(lambda e: e.tensor_scalar(L(LS_T1), L(LS_SUM), 1.0 / 1024.0, None, OP.mult), sv, 1, dup=True)
        VW(sv, sv.c)
        VE(lambda e: e.tensor_add(L(LS_T2), L(LS_S1), L(LS_S2)), sv, 1, dup=True)
        VW(sv, sv.c)
        VE(lambda e: e.tensor_scalar(L(LS_T2), L(LS_T2), 0.0, None, OP.add) if False else
           e.tensor_scalar(L(LS_S1), L(LS_T2), 1.0 / 1024.0, None, OP.mult), sv, 1, dup=True)
        VW(sv, sv.c)
        VE(lambda e: e.tensor_tensor(L(LS_S2), L(LS_T1), L(LS_T1), OP.mult), sv, 1, dup=True)
        VW(sv, sv.c)
        VE(lambda e: e.tensor_sub(L(LS_T2), L(LS_S1), L(LS_S2)), sv, 1, dup=True)
        VW(sv, sv.c)
        VE(lambda e: e.tensor_scalar(L(LS_T3), L(LS_T2), EPS_LN, None, OP.add), sv, 1, dup=True)
        ln_v = sv.c
        SW(sv, ln_v)
        SE(lambda e: e.sqrt(L(LS_S1), L(LS_T3)), ss, 1, dup=True)
        VW(ss, ss.c)
        VE(lambda e: e.reciprocal(L(LS_RS), L(LS_S1)), sv, 1, dup=True)
        VW(sv, sv.c)
        VE(lambda e: e.tensor_tensor(L(LS_T2), L(LS_T1), L(LS_RS), OP.mult), sv, 1, dup=True)
        VW(sv, sv.c)
        VE(lambda e: e.tensor_scalar(L(LS_NM), L(LS_T2), -1.0, None, OP.mult), sv, 1, dup=True)
        VW(sv, sv.c)
        VE(lambda e: e.tensor_sub(L(LS_T2), L(LS_MAX), L(LS_T1)), sv, 1, dup=True)
        VW(sv, sv.c)
        VE(lambda e: e.tensor_sub(L(LS_T3), L(LS_T1), L(LS_MIN)), sv, 1, dup=True)
        VW(sv, sv.c)
        VE(lambda e: e.tensor_tensor(L(LS_XM), L(LS_T2), L(LS_T3), OP.max), sv, 1, dup=True)
        VW(sv, sv.c)
        VE(lambda e: e.tensor_tensor(L(LS_T2), L(LS_XM), L(LS_RS), OP.mult), sv, 1, dup=True)
        VW(sv, sv.c)
        VE(lambda e: e.tensor_reduce(scal[:, MC_XN:MC_XN + 1], L(LS_T2),
                                     AX, OP.max), sv, 1, dup=True)
        VW(sv, sv.c)

        VW(sg, wld_g)
        for i in range(4):
            for ch in range(4):
                view = wraw[i * 4 + ch]
                if ch == 0:
                    VE(lambda e, view=view, i=i: e.tensor_reduce(
                        scal[:, 1 + i:2 + i], view, AX, OP.max, apply_absolute_value=True), sv, 1, dup=True)
                    VW(sv, sv.c)
                else:
                    VE(lambda e, view=view: e.tensor_reduce(
                        stats[:, 0:1], view, AX, OP.max, apply_absolute_value=True), sv, 1, dup=True)
                    VW(sv, sv.c)
                    VE(lambda e, i=i: e.tensor_tensor(
                        scal[:, 1 + i:2 + i], scal[:, 1 + i:2 + i], stats[:, 0:1], OP.max), sv, 1, dup=True)
                    VW(sv, sv.c)
        VE(lambda e: e.tensor_copy(tmp1[:, 0:5], scal[:, 0:5]), sv, 1)
        maxes_v = sv.c

        TW(sv, maxes_v)
        TW(sy, cst_y)
        TE(lambda e: e.matmul(tp32[0:5, 0:128], tmp1[:, 0:5], ident32[:, :],
                              start=True, stop=True, is_transpose=True), s2, 1)
        VW(s2, s2.c)
        VE(lambda e: e.tensor_reduce(stats[0:5, 0:1], tp32[0:5, 0:128], AX, OP.max), sv, 1, dup=True)
        VW(sv, sv.c)
        red_v = sv.c
        GW(sv, red_v)
        GE(lambda e: e.dma_start(out=cc1_in.ap()[0:5], in_=stats[0:5, 0:1]), sg, 16)
        GW(sg, sg.c)
        GE(lambda e: e.collective_compute(
            "AllReduce", OP.max, replica_groups=[list(range(8))],
            ins=[cc1_in.ap().opt()], outs=[cc1_out.ap().opt()]), scc, 1)
        GW(scc, scc.c)
        GE(lambda e: e.dma_start(out=srow[0:1, 0:5], in_=cc1_out.ap()[0:5]), sg, 16)
        cc1_g = sg.c

        VW(sg, cc1_g)
        for i in range(5):
            VE(lambda e, i=i: e.tensor_scalar(
                srow[0:1, 8 + i:9 + i], srow[0:1, i:i + 1], 1.0 / 127.0, 1e-8,
                OP.mult, OP.max), sv, 1, dup=True)
            VW(sv, sv.c)
        VE(lambda e: e.reciprocal(mrow[0:1, 0:5], srow[0:1, 8:13]), sv, 1, dup=True)
        VW(sv, sv.c)
        VE(lambda e: e.tensor_copy(mrow[0:1, 8:9], mrow[0:1, 0:1]), sv, 1, dup=True)
        VW(sv, sv.c)
        for i, scol in ((0, 9), (1, 10), (2, 11)):
            VE(lambda e, i=i, scol=scol: e.tensor_single_scalar(
                mrow[0:1, scol:scol + 1], srow[0:1, 9 + i:10 + i], srow[0:1, 8:9], OP.mult), sv, 1, dup=True)
            VW(sv, sv.c)
        VE(lambda e: e.tensor_copy(mrow[0:1, 12:16], mrow[0:1, 1:5]), sv, 1, dup=True)
        VW(sv, sv.c)
        scl_v = sv.c
        TW(sv, scl_v)
        TE(lambda e: e.matmul(tp32[:, 0:8], onesr[0:1, :], mrow[0:1, 8:16],
                              start=True, stop=True), s2, 1)
        VW(s2, s2.c)
        VE(lambda e: e.tensor_copy(sbc[:, 0:8], tp32[:, 0:8]), sv, 1, dup=True)
        VW(sv, sv.c)
        sbc_v = sv.c

        YW(sv, sbc_v)
        YE(lambda e: e.dma_start(out=tmp2[0:1, :], in_=bv_d), sy, 16)
        TW(sy, sy.c)
        TW(sv, sbc_v)
        TE(lambda e: e.matmul(tp32[:, 0:512], onesr[0:1, :], tmp2[0:1, :],
                              start=True, stop=True), s2, 1)
        VW(s2, s2.c)
        VE(lambda e: e.tensor_copy(bvrep[:, :], tp32[:, 0:512]), sv, 1)

        for i, (rw_col, wT) in enumerate(((SC_RWQ, wqT), (SC_RWK, wkT),
                                          (SC_RWV, wvT), (SC_RWO, woT))):
            for ch in range(4):
                raw = wraw[i * 4 + ch]
                if not (i == 0 and ch == 0):
                    VW(s2, s2.c)
                VE(lambda e, raw=raw, rw_col=rw_col: e.tensor_scalar(
                    q16[:, :], raw, sbc[:, rw_col:rw_col + 1],
                    cpool[:, CP_1536:CP_1536 + 1], OP.mult, OP.add), sv, 1)
                VE(lambda e: e.tensor_scalar(qtm[:, :], q16[:, :], -1536.0, None, OP.add), sv, 1)
                qv = sv.c
                TW(sv, qv)
                for c2 in range(8):
                    TE(lambda e, c2=c2: e.matmul(
                        tp0[:, c2 * 128:(c2 + 1) * 128], qtm[:, c2 * 128:(c2 + 1) * 128],
                        identv, start=(c2 == 0), stop=(c2 == 7), is_transpose=True),
                        s2 if c2 == 7 else None, 1)
                tv = s2.c
                VW(s2, tv)
                dst = bass.AP(wt, wT.offset + ch * 128,
                              [[wt.shape[1], 128], [512, 8], [1, 128]])
                VE(lambda e, dst=dst: e.tensor_copy(
                    dst, tp0.rearrange("p (a b) -> p a b", b=128)), sv, 1)
        wq_done_v = sv.c

        hi2 = [big2[:, ha + p * 6144 + 0:ha + p * 6144 + 1024] for p in range(2)]
        lo2 = [big2[:, ha + p * 6144 + 1024:ha + p * 6144 + 2048] for p in range(2)]
        hf2 = [big2[:, ha + p * 6144 + 2048:ha + p * 6144 + 3072] for p in range(2)]
        lf2 = [big2[:, ha + p * 6144 + 3072:ha + p * 6144 + 4096] for p in range(2)]
        qm8 = [big2[:, ha + p * 6144 + 4096:ha + p * 6144 + 5120] for p in range(2)]
        q68 = [big2[:, ha + p * 6144 + 5120:ha + p * 6144 + 6144].bitcast(F16) for p in range(2)]
        q64 = [hf2[p].bitcast(F16) for p in range(2)]
        tpv = [[tp0, tp1, tp2],
               [pmm[:, 5, :].bitcast(BF16), pmm[:, 0, :].bitcast(BF16), pmm[:, 1, :].bitcast(BF16)]]
        b6t = pmm[:, 6, :].bitcast(BF16)
        bsv, bss, bs2, bsq, bsy = sv.c, ss.c, s2.c, sq.c, sy.c
        ldy, normv, lov, shi, t1t, e1v, t2t, v2r, sdq, t3t, e2v, spq, dbgq = ({} for _ in range(13))

        def g(d, k, base):
            return d.get(k, base)
        xq4_spill_q = spq
        for j in range(NT + 2):
            p = j % 2
            p1 = (j - 1) % 2
            if j < 2:
                YW(sv, wq_done_v)
                YE(lambda e, j=j: e.dma_start(
                    out=xnt[:, (j % 2) * D:((j % 2) + 1) * D],
                    in_=xb[j * 128:(j + 1) * 128, :]), sy, 16)
                ldy[j] = sy.c
            if j < NT:
                xnb = xnt[:, p * D:(p + 1) * D]
                VW(sy, ldy[j])
                VE(lambda e, xnb=xnb, j=j: e.tensor_scalar(
                    xnb, xnb, lnst[:, LS_RS + j:LS_RS + j + 1], lnst[:, LS_NM + j:LS_NM + j + 1],
                    OP.mult, OP.add), sv, 1)
                normv[j] = sv.c
                SW(sv, normv[j])
                SW(s2, g(t1t, j - 2, bs2))
                SE(lambda e, xnb=xnb, p=p: e.activation(
                    q68[p][:, :], xnb, AF.Identity,
                    bias=cpool[:, CP_1536:CP_1536 + 1], scale=sbc[:, SC_RX:SC_RX + 1]), ss, 1)
                SE(lambda e, p=p: e.activation(
                    qm8[p][:, :], q68[p][:, :], AF.Identity,
                    bias=cpool[:, CP_N1536:CP_N1536 + 1], scale=cpool[:, CP_1:CP_1 + 1]), ss, 1)
                SE(lambda e, xnb=xnb, p=p: e.activation(
                    hi2[p][:, :], xnb, AF.Identity,
                    bias=0.0, scale=1.0), ss, 1)
                shi[j] = ss.c
                if debug:
                    SE(lambda e, j=j, xnb=xnb: e.dma_start(
                        out=xn_sp.ap()[j * 128:(j + 1) * 128, :], in_=xnb), sq, 16)
                    dbgq[j] = sq.c
                VW(ss, shi[j])
                VW(sq, g(spq, j - 2, bsq))
                VE(lambda e, xnb=xnb, p=p: e.scalar_tensor_tensor(
                    lo2[p], xnb, 1.0, hi2[p], OP.mult, OP.subtract), sv, 1)
                lov[j] = sv.c
                if j + 2 < NT:
                    jn = j + 2
                    YW(sv, lov[j])
                    YW(ss, shi[j])
                    if debug:
                        YW(sq, dbgq[j])
                    YE(lambda e, jn=jn: e.dma_start(
                        out=xnt[:, (jn % 2) * D:((jn % 2) + 1) * D],
                        in_=xb[jn * 128:(jn + 1) * 128, :]), sy, 16)
                    ldy[jn] = sy.c
                TW(ss, shi[j])
                TW(sv, lov[j])
                TW(sv, g(e1v, j - 2, bsv))
                for bi, srct in enumerate((qm8[p], hi2[p], lo2[p])):
                    for c2 in range(8):
                        TE(lambda e, bi=bi, c2=c2, srct=srct, p=p: e.matmul(
                            tpv[p][bi][:, c2 * 128:(c2 + 1) * 128], srct[:, c2 * 128:(c2 + 1) * 128],
                            identv, start=(c2 == 0), stop=(c2 == 7), is_transpose=True),
                            s2 if (bi == 2 and c2 == 7) else None, 1)
                t1t[j] = s2.c
            if j - 2 >= 0:
                j2 = j - 2
                so = p * 8
                VW(s2, t2t[j2])
                VE(lambda e, so=so: e.tensor_reduce(stats[:, so:so + 1], mm_b[0], AX, OP.max,
                                                    apply_absolute_value=True), sv, 1, dup=True)
                VW(sv, sv.c)
                VE(lambda e, so=so: e.tensor_reduce(stats[:, so + 1:so + 2], mm_b[1], AX, OP.max,
                                                    apply_absolute_value=True), sv, 1, dup=True)
                VW(sv, sv.c)
                VE(lambda e, so=so: e.tensor_tensor(stats[:, so + 2:so + 3], stats[:, so:so + 1],
                                                    stats[:, so + 1:so + 2], OP.max), sv, 1, dup=True)
                VW(sv, sv.c)
                VE(lambda e, so=so: e.tensor_scalar(stats[:, so + 3:so + 4], stats[:, so + 2:so + 3],
                                                    1e-5, 1.0 / 7.0, OP.max, OP.mult), sv, 1, dup=True)
                VW(sv, sv.c)
                VE(lambda e, so=so, j2=j2: e.tensor_copy(rs4[:, j2:j2 + 1], stats[:, so + 3:so + 4]),
                   sv, 1, dup=True)
                VW(sv, sv.c)
                VE(lambda e, so=so: e.reciprocal(stats[:, so + 4:so + 5], stats[:, so + 3:so + 4]),
                   sv, 1, dup=True)
                VW(sv, sv.c)
                v2r[j2] = sv.c
                SW(sv, v2r[j2])
                for jh in range(2):
                    SE(lambda e, jh=jh, so=so, p=p: e.activation(
                        q64[p][:, jh * 512:(jh + 1) * 512], mm_b[jh], AF.Identity,
                        bias=cpool[:, CP_1536:CP_1536 + 1], scale=stats[:, so + 4:so + 5]), ss, 1)
                SE(lambda e, p=p: e.activation(
                    lf2[p][:, :], q64[p][:, :], AF.Identity,
                    bias=cpool[:, CP_N1536:CP_N1536 + 1], scale=cpool[:, CP_1:CP_1 + 1]), ss, 1)
                sdq[j2] = ss.c
                TW(ss, sdq[j2])
                for c2 in range(8):
                    TE(lambda e, c2=c2, p=p: e.matmul(
                        b6t[:, c2 * 128:(c2 + 1) * 128], lf2[p][:, c2 * 128:(c2 + 1) * 128],
                        identv, start=(c2 == 0), stop=(c2 == 7), is_transpose=True),
                        s2 if c2 == 7 else None, 1)
                t3t[j2] = s2.c
            if j - 1 >= 0 and j - 1 < NT:
                j1 = j - 1
                VW(s2, t1t[j1])
                VW(s2, g(t3t, j1 - 2, bs2))
                dst8 = bass.AP(big1, j1 * 128, [[32832, 128], [T, 8], [1, 128]])
                VE(lambda e, dst8=dst8, p1=p1: e.tensor_copy(
                    dst8, tpv[p1][0].rearrange("p (a b) -> p a b", b=128)), sv, 1)
                VE(lambda e, p1=p1: e.tensor_copy(hf2[p1], tpv[p1][1]), sv, 1)
                VE(lambda e, p1=p1: e.tensor_copy(lf2[p1], tpv[p1][2]), sv, 1)
                e1v[j1] = sv.c
            if j - 2 >= 0:
                j2 = j - 2
                VW(s2, t3t[j2])
                if j < NT:
                    VW(s2, t1t[j])
                VE(lambda e, p=p: e.tensor_copy(lo2[p], b6t[:, :]), sv, 1)
                e2v[j2] = sv.c
            if j - 1 >= 0 and j - 1 < NT:
                j1 = j - 1
                TW(sv, e1v[j1])
                TW(sv, g(e2v, j1 - 1, bsv))
                TW(ss, g(sdq, j1 - 1, bss))
                n_mm = 0
                for jh in range(2):
                    for c2 in range(8):
                        for srct in (hf2[p1], lf2[p1]):
                            last = (n_mm == 31)
                            TE(lambda e, jh=jh, c2=c2, srct=srct, first=(n_mm % 16 == 0), last=last:
                               e.matmul(mm_b[jh], srct[:, c2 * 128:(c2 + 1) * 128],
                                        hbuf[:, c2 * 1024 + jh * 512: c2 * 1024 + jh * 512 + 512],
                                        start=first, stop=(n_mm % 16 == 15)),
                               s2 if last else None, 1)
                            n_mm += 1
                t2t[j1] = s2.c
            if j - 2 >= 0:
                j2 = j - 2
                SW(sv, e2v[j2])
                SE(lambda e, j2=j2, p=p: e.dma_start(
                    out=bass.AP(xq4_sp, j2 * 128, [[T, 128], [128 * T, 8], [1, 128]]),
                    in_=lo2[p].rearrange("p (a b) -> p a b", b=128)), sq, 16)
                spq[j2] = sq.c

        SW(sv, sv.c)
        for ch in range(NCH):
            SE(lambda e, ch=ch: e.dma_start(out=xq8_sp.ap()[ch], in_=xq8f[ch]), sq, 16)

        TW(ss, ss.c)
        bank_free = [sv.c, sv.c]
        stg_free = [sq.c, sq.c]
        for j in range(NT):
            bank = mm_b[j % 2]
            TW(sv, bank_free[j % 2])
            for ch in range(NCH):
                TE(lambda e, bank=bank, ch=ch, j=j: e.matmul(
                    bank, xq8f[ch][:, j * 128:(j + 1) * 128], wvT[:, ch * 512:(ch + 1) * 512],
                    start=(ch == 0), stop=(ch == NCH - 1)),
                    s2 if ch == NCH - 1 else None, 1)
            vt = s2.c
            VW(s2, vt)
            VE(lambda e, bank=bank: e.scalar_tensor_tensor(
                tmp1[:, :], bank, sbc[:, SC_AV:SC_AV + 1], bvrep[:, :], OP.mult, OP.add), sv, 1)
            bank_free[j % 2] = sv.c
            VW(sq, stg_free[j % 2])
            so = (j % 2) * 1032
            hidst = bass.AP(vstage, so, [[2 * 1032, 128], [129, 8], [1, 64]])
            hisrc = tmp1.rearrange("p (h c) -> p h c", c=64)
            VE(lambda e, hidst=hidst, hisrc=hisrc: e.tensor_copy(hidst, hisrc), sv, 1)
            lodst = bass.AP(vstage, so + 65, [[2 * 1032, 128], [129, 8], [1, 64]])
            VE(lambda e, lodst=lodst, hisrc=hisrc, hidst=hidst: e.scalar_tensor_tensor(
                lodst, hisrc, 1.0, hidst, OP.mult, OP.subtract), sv, 1)
            onedst = bass.AP(vstage, so + 64, [[2 * 1032, 128], [129, 8], [1, 1]])
            VE(lambda e, onedst=onedst: e.memset(onedst, 1.0), sv, 1)
            spl_v = sv.c
            SW(sv, spl_v)
            SE(lambda e, j=j, so=so: e.dma_start(
                out=bass.AP(vaug_sp, j * 129, [[NT * 129, 128], [128 * NT * 129, 8], [1, 129]]),
                in_=bass.AP(vstage, so, [[2 * 1032, 128], [129, 8], [1, 129]])), sq, 16)
            stg_free[j % 2] = sq.c
        if debug:
            GW(sv, sv.c)
            GW(sq, sq.c)
            GE(lambda e: e.dma_start(out=dbg["lnst"], in_=lnst[:, :]), sg, 16)
            GE(lambda e: e.dma_start(out=dbg["xn"], in_=xn_sp.ap()), sg, 16)
            GW(sg, sg.c)

    def phaseB(cur, eng):
        mk = mkwalk(cur, eng)
        VE, VW = mk("vector")
        TE, TW = mk("tensor")
        SE, SW = mk("scalar")
        YE, YW = mk("sync")
        GE, GW = mk("gpsimd")

        stacks_free_t = s2.c
        yhead_free_y = sy.c
        va_free = [s2.c, s2.c]
        va_y = [sy.c, sy.c]
        pair_sq_ss = ss.c

        YW(sq, sq.c)
        GW(sq, sq.c)

        for hp in range(4):
            for proj in range(2):
                wT = wqT if proj == 0 else wkT
                acol = SC_AQ if proj == 0 else SC_AK
                bcol = 0 + hp if proj == 0 else 4 + hp
                dst_hi = pairQH if proj == 0 else pairKH
                dst_lo = pairQL if proj == 0 else pairKL
                stg_t = [s2.c, s2.c]
                for ch in range(NCH):
                    stg = xq8stg[ch % 2]
                    YW(s2, stg_t[ch % 2])
                    if ch == 0:
                        YW(ss, ss.c)
                        YW(sv, sv.c)
                    YE(lambda e, ch=ch, stg=stg: e.dma_start(out=stg, in_=xq8_sp.ap()[ch]), sy, 16)
                    TW(sy, sy.c)
                    if ch == 0:
                        TW(ss, ss.c)
                        TW(sv, sv.c)
                    for ts in range(NSP):
                        TE(lambda e, ts=ts, ch=ch, stg=stg, wT=wT, hp=hp: e.matmul(
                            pmm[:, ts, :], wT[:, ch * 512 + hp * 128: ch * 512 + hp * 128 + 128],
                            stg[:, ts * 512:(ts + 1) * 512],
                            start=(ch == 0), stop=(ch == NCH - 1)),
                            s2 if ts == NSP - 1 else None, 1)
                    stg_t[ch % 2] = s2.c
                proj_t = s2.c
                for ts in range(NSP):
                    SW(s2, proj_t)
                    if ts > 0:
                        SW(sv, sv.c)
                    SE(lambda e, ts=ts, acol=acol, bcol=bcol: e.activation(
                        tmp1[:, :], pmm[:, ts, :], AF.Identity,
                        bias=brow[:, bcol:bcol + 1], scale=sbc[:, acol:acol + 1]), ss, 1)
                    VW(ss, ss.c)
                    if ts == 0 and proj == 0:
                        VW(sg, sg.c)
                        VW(ss, pair_sq_ss)
                        VW(sy, sy.c)
                    VE(lambda e, ts=ts, dst_hi=dst_hi: e.tensor_copy(
                        dst_hi[:, ts * 512:(ts + 1) * 512], tmp1[:, :]), sv, 1)
                    VE(lambda e, ts=ts, dst_hi=dst_hi, dst_lo=dst_lo: e.scalar_tensor_tensor(
                        dst_lo[:, ts * 512:(ts + 1) * 512], tmp1[:, :], 1.0,
                        dst_hi[:, ts * 512:(ts + 1) * 512], OP.mult, OP.subtract), sv, 1)
            epi_v = sv.c

            SW(sv, epi_v)
            SE(lambda e: e.activation(sqq, pairQH, AF.Square, bias=0.0, scale=1.0), ss, 1)
            sq_q = ss.c
            for src, outc in ((0, 0), (1, 1)):
                if src == 1:
                    SW(s2, s2.c)
                    SE(lambda e: e.activation(sqq, pairKH, AF.Square, bias=0.0, scale=1.0), ss, 1)
                    pair_sq_ss = ss.c
                for ts in range(NSP):
                    TW(ss, ss.c)
                    if ts > 0 or src == 1:
                        TW(sv, sv.c)
                    TE(lambda e, ts=ts: e.matmul(
                        tpB[0:2, 0:512], selAB[:, 0:2], sqq[:, ts * 512:(ts + 1) * 512],
                        start=True, stop=True), s2, 1)
                    VW(s2, s2.c)
                    VE(lambda e: e.tensor_reduce(stats[0:2, 0:1], tpB[0:2, 0:512], AX, OP.max), sv, 1, dup=True)
                    VW(sv, sv.c)
                    if ts == 0:
                        VE(lambda e, outc=outc: e.tensor_copy(qn2[0:2, outc:outc + 1],
                                                              stats[0:2, 0:1]), sv, 1, dup=True)
                        VW(sv, sv.c)
                    else:
                        VE(lambda e, outc=outc: e.tensor_tensor(
                            qn2[0:2, outc:outc + 1], qn2[0:2, outc:outc + 1],
                            stats[0:2, 0:1], OP.max), sv, 1, dup=True)
                        VW(sv, sv.c)
            VE(lambda e: e.tensor_tensor(qn2[0:2, 2:3], qn2[0:2, 0:1], qn2[0:2, 1:2], OP.mult), sv, 1, dup=True)
            VW(sv, sv.c)
            m_v = sv.c
            SW(sv, m_v)
            SE(lambda e: e.sqrt(qn2[0:2, 3:4], qn2[0:2, 2:3]), ss, 1, dup=True)
            VW(ss, ss.c)
            VE(lambda e: e.tensor_scalar(qn2[0:2, 3:4], qn2[0:2, 3:4], -0.1275, None, OP.mult), sv, 1, dup=True)
            VW(sv, sv.c)
            mn_v = sv.c
            TW(sv, mn_v)
            TE(lambda e: e.matmul(tpB[0:1, 0:2], qn2[0:2, 3:4], ident32[0:2, 0:2],
                                  start=True, stop=True, is_transpose=True), s2, 1)
            VW(s2, s2.c)
            VE(lambda e: e.tensor_copy(mrow[0:1, 0:2], tpB[0:1, 0:2]), sv, 1, dup=True)
            VW(sv, sv.c)
            mr_v = sv.c
            TW(sv, mr_v)
            TE(lambda e: e.matmul(tpB[:, 8:10], onesr[0:1, :], mrow[0:1, 0:2],
                                  start=True, stop=True), s2, 1)
            VW(s2, s2.c)
            VE(lambda e, hp=hp: e.tensor_copy(mbc[:, 2 * hp:2 * hp + 2], tpB[:, 8:10]), sv, 1, dup=True)
            VW(sv, sv.c)

            stacks_g_h = [None, None]
            GW(sv, epi_v)
            GW(s2, stacks_free_t)
            for h01 in range(2):
                r0_ = 64 * h01
                stK, stQH, stQL = stKs[h01], stQHs[h01], stQLs[h01]
                GE(lambda e, r0_=r0_, stK=stK: e.dma_start(out=stK[0:64, :], in_=pairKH[r0_:r0_ + 64, :]), sg, 16)
                GE(lambda e, r0_=r0_, stK=stK: e.dma_start(out=stK[64:128, :], in_=pairKL[r0_:r0_ + 64, :]), sg, 16)
                GE(lambda e, r0_=r0_, stQH=stQH: e.dma_start(out=stQH[0:64, :], in_=pairQH[r0_:r0_ + 64, :]), sg, 16)
                GE(lambda e, r0_=r0_, stQH=stQH: e.dma_start(out=stQH[64:128, :], in_=pairQH[r0_:r0_ + 64, :]), sg, 16)
                GE(lambda e, r0_=r0_, stQL=stQL: e.dma_start(out=stQL[0:64, :], in_=pairQL[r0_:r0_ + 64, :]), sg, 16)
                GE(lambda e, r0_=r0_, stQL=stQL: e.dma_start(out=stQL[64:128, :], in_=pairQL[r0_:r0_ + 64, :]), sg, 16)
                stacks_g_h[h01] = sg.c
            YW(s2, va_free[0])
            YE(lambda e, hp=hp: e.dma_start(out=vaugs[0], in_=vaug_sp.ap()[2 * hp]), sy, 16)
            va_y[0] = sy.c
            YW(sg, stacks_g_h[1])
            YW(ss, pair_sq_ss)
            YW(s2, va_free[1])
            YE(lambda e, hp=hp: e.dma_start(out=vaugs[1], in_=vaug_sp.ap()[2 * hp + 1]), sy, 16)
            va_y[1] = sy.c

            for h01 in range(2):
                hloc = 2 * hp + h01
                stK, stQH, stQL = stKs[h01], stQHs[h01], stQLs[h01]
                vaug = vaugs[h01]
                stacks_g = stacks_g_h[h01]
                vaug_y = va_y[h01]

                exp_done = {}
                sub_done = {}
                pv_done = {}
                comb_done = sv.c
                sc_free = [ss.c, ss.c, ss.c]
                slot_pv = [s2.c] * RINGN
                pf_sub = [sv.c, sv.c]
                TW(sg, stacks_g)
                TW(sy, vaug_y)
                pva = [pmm[:, 6 + sb // 2, (sb % 2) * 256:(sb % 2) * 256 + 129] for sb in range(4)]
                denb = lnst[:, 0:NT]
                denr = lnst[:, NT:2 * NT]
                started_bank = {}
                qk_t = {}
                comb_of_stripe = {}

                def emit_qk2(j, kt, pp):
                    c = max(0, kt - 4 * j)
                    scb = pmm[:, 2 * pp + (kt % 2), :]
                    diag = (kt >= 4 * j)
                    TW(ss, sc_free[pp])
                    TE(lambda e, scb=scb, kt=kt, j=j: e.matmul(
                        scb[0:128, 0:512], stK[:, kt * 128:(kt + 1) * 128],
                        stQH[:, j * 512:(j + 1) * 512],
                        start=True, stop=False), None)
                    TE(lambda e, scb=scb, kt=kt, j=j, diag=diag: e.matmul(
                        scb[0:128, 0:512], stK[:, kt * 128:(kt + 1) * 128],
                        stQL[:, j * 512:(j + 1) * 512],
                        start=False, stop=(not diag)),
                        s2 if not diag else None, 1)
                    if diag:
                        TE(lambda e, scb=scb, c=c: e.matmul(
                            scb[0:128, 0:512], Av, Bv[c][:, 0:512],
                            start=False, stop=True), s2, 1)
                    qk_t[(j, kt)] = s2.c

                def emit_tail2(j, bb, pp, p2, gb):
                    kt0, kt1 = 2 * bb, 2 * bb + 1
                    scpair = pmm[:, 2 * pp:2 * pp + 2, :]
                    slot = (2 * gb) % RINGN
                    SW(s2, qk_t[(j, kt1)])
                    SW(s2, slot_pv[slot])
                    SW(s2, slot_pv[slot + 1])
                    SW(sv, pf_sub[p2])
                    phpair = big2[:, r0 + slot * 512: r0 + (slot + 2) * 512]
                    plpair = big2[:, r1 + slot * 512: r1 + (slot + 2) * 512]
                    SE(lambda e, scpair=scpair, phpair=phpair, hloc=hloc: e.activation(
                        phpair, scpair, AF.Exp,
                        bias=mbc[:, hloc:hloc + 1], scale=cpool[:, CP_125:CP_125 + 1]), ss, 1)
                    SE(lambda e, scpair=scpair, p2=p2, hloc=hloc: e.activation(
                        pf32[p2][:, :], scpair, AF.Exp,
                        bias=mbc[:, hloc:hloc + 1], scale=cpool[:, CP_125:CP_125 + 1]), ss, 1)
                    exp_done[(j, bb)] = ss.c
                    sc_free[pp] = ss.c
                    VW(ss, exp_done[(j, bb)])
                    VW(s2, slot_pv[slot])
                    VW(s2, slot_pv[slot + 1])
                    VE(lambda e, p2=p2, phpair=phpair, plpair=plpair: e.scalar_tensor_tensor(
                        plpair, pf32[p2][:, :], 1.0, phpair, OP.mult, OP.subtract), sv, 1)
                    sub_done[(j, bb)] = sv.c
                    pf_sub[p2] = sv.c
                    TW(sv, sub_done[(j, bb)])
                    if bb == 0:
                        TW(sv, comb_of_stripe.get(j - 1, comb_done))
                    for kt in (kt0, kt1):
                        c = max(0, kt - 4 * j)
                        slot_k = (2 * gb + (kt % 2)) % RINGN
                        for sb in range(c, 4):
                            bk_i = sb // 2
                            first = not started_bank.get((j, bk_i), False)
                            started_bank[(j, bk_i)] = True
                            stop_b = (sb % 2 == 1) and (kt == 4 * j + sb)
                            TE(lambda e, sb=sb, slot_k=slot_k, kt=kt, first=first: e.matmul(
                                pva[sb], ringPh[slot_k][:, sb * 128:(sb + 1) * 128],
                                vaug[:, kt * 129:(kt + 1) * 129],
                                start=first, stop=False), None)
                            TE(lambda e, sb=sb, slot_k=slot_k, kt=kt, stop_b=stop_b: e.matmul(
                                pva[sb][:, 0:65], ringPl[slot_k][:, sb * 128:(sb + 1) * 128],
                                vaug[:, kt * 129:kt * 129 + 65],
                                start=False, stop=stop_b),
                                s2 if sb == 3 else None, 1)
                        pv_done[(j, kt)] = s2.c
                        slot_pv[slot_k] = s2.c

                def emit_combines(j):
                    nonlocal comb_done
                    for sb in range(4):
                        qt = 4 * j + sb
                        VW(s2, pv_done[(j, 4 * j + (sb | 1))])
                        if j == 0 and sb == 0:
                            VW(sy, yhead_free_y)
                        VE(lambda e, sb=sb: e.tensor_copy(
                            tmp2[:, 64:128], pva[sb][:, 65:129]), sv, 1)
                        VE(lambda e, sb=sb, qt=qt: e.tensor_add(
                            yhead[:, qt * 64:(qt + 1) * 64], tmp2[:, 64:128],
                            pva[sb][:, 0:64]), sv, 1)
                        VE(lambda e, sb=sb, qt=qt: e.tensor_copy(
                            denb[:, qt:qt + 1], pva[sb][:, 64:65]), sv, 1, dup=True)
                    comb_done = sv.c
                    comb_of_stripe[j] = sv.c

                seq = [(j, bb) for j in range(NT // 4) for bb in range(2 * j + 2)]
                for i, (j, bb) in enumerate(seq):
                    pp = i % 3
                    p2 = i % 2
                    emit_qk2(j, 2 * bb, pp)
                    emit_qk2(j, 2 * bb + 1, pp)
                    if i >= 2:
                        j2, bb2 = seq[i - 2]
                        emit_tail2(j2, bb2, (i - 2) % 3, (i - 2) % 2, i - 2)
                        if bb2 == 2 * j2 + 1:
                            emit_combines(j2)
                for i in (len(seq) - 2, len(seq) - 1):
                    j2, bb2 = seq[i]
                    emit_tail2(j2, bb2, i % 3, i % 2, i)
                    if bb2 == 2 * j2 + 1:
                        emit_combines(j2)

                VW(sv, comb_done)
                VE(lambda e: e.reciprocal(denr, denb), sv, 1, dup=True)
                VW(sv, sv.c)
                for qt in range(NT):
                    VE(lambda e, qt=qt: e.tensor_scalar(
                        yhead[:, qt * 64:(qt + 1) * 64], yhead[:, qt * 64:(qt + 1) * 64],
                        denr[:, qt:qt + 1], None, OP.mult), sv, 1)
                VE(lambda e: e.tensor_reduce(stats[:, 1:2], yhead[:, 0:NT * 64], AX, OP.max,
                                             apply_absolute_value=True), sv, 1, dup=True)
                VW(sv, sv.c)
                if hloc == 0:
                    VE(lambda e: e.tensor_copy(scal[:, MC_Y:MC_Y + 1], stats[:, 1:2]), sv, 1, dup=True)
                else:
                    VE(lambda e: e.tensor_tensor(
                        scal[:, MC_Y:MC_Y + 1], scal[:, MC_Y:MC_Y + 1],
                        stats[:, 1:2], OP.max), sv, 1, dup=True)
                VW(sv, sv.c)
                YW(sv, sv.c)
                YE(lambda e, hloc=hloc: e.dma_start(out=y_sp.ap()[hloc], in_=yhead), sy, 16)
                yhead_free_y = sy.c
                stacks_free_t = s2.c
                va_free[h01] = s2.c

    def phaseC(cur, eng):
        mk = mkwalk(cur, eng)
        VE, VW = mk("vector")
        TE, TW = mk("tensor")
        SE, SW = mk("scalar")
        YE, YW = mk("sync")
        GE, GW = mk("gpsimd")

        VW(sy, sy.c)
        VE(lambda e: e.tensor_copy(tmp1[:, 0:1], scal[:, MC_Y:MC_Y + 1]), sv, 1)
        ym_v = sv.c
        TW(sv, ym_v)
        TE(lambda e: e.matmul(tp32[0:1, 0:128], tmp1[:, 0:1], ident32[:, :],
                              start=True, stop=True, is_transpose=True), s2, 1)
        VW(s2, s2.c)
        VE(lambda e: e.tensor_reduce(stats[0:1, 0:1], tp32[0:1, 0:128], AX, OP.max), sv, 1, dup=True)
        VW(sv, sv.c)
        red_v = sv.c
        GW(sv, red_v)
        GE(lambda e: e.dma_start(out=cc2_in.ap()[0:1], in_=stats[0:1, 0:1]), sg, 16)
        GW(sg, sg.c)
        GE(lambda e: e.collective_compute(
            "AllReduce", OP.max, replica_groups=[list(range(8))],
            ins=[cc2_in.ap().opt()], outs=[cc2_out.ap().opt()]), scc, 1)
        GW(scc, scc.c)
        GE(lambda e: e.dma_start(out=srow[0:1, 14:15], in_=cc2_out.ap()[0:1]), sg, 16)
        cc2_g = sg.c

        if CSTEP <= 1:
            return
        for ch in range(4):
            GE(lambda e, ch=ch: e.dma_start(out=wfraw[ch], in_=wf_d[ch * 128:(ch + 1) * 128, :]), sg, 16)
        wf_g = sg.c

        YE(lambda e: e.dma_start(out=tmp2[0:1, :], in_=bo_d), sy, 16)
        bo_y = sy.c
        if CSTEP <= 11:
            return

        VW(sg, cc2_g)
        VE(lambda e: e.tensor_scalar(srow[0:1, 15:16], srow[0:1, 14:15], 1.0 / 127.0, 1e-8,
                                     OP.mult, OP.max), sv, 1, dup=True)
        VW(sv, sv.c)
        VE(lambda e: e.reciprocal(mrow[0:1, 8:9], srow[0:1, 15:16]), sv, 1, dup=True)
        VW(sv, sv.c)
        VE(lambda e: e.tensor_single_scalar(
            mrow[0:1, 9:10], srow[0:1, 15:16], srow[0:1, 12:13], OP.mult), sv, 1, dup=True)
        VW(sv, sv.c)
        ry_v = sv.c
        TW(sv, ry_v)
        TE(lambda e: e.matmul(tp32[:, 0:2], onesr[0:1, :], mrow[0:1, 8:10],
                              start=True, stop=True), s2, 1)
        VW(s2, s2.c)
        VE(lambda e: e.tensor_copy(sbc[:, SC_RY:SC_RY + 2], tp32[:, 0:2]), sv, 1, dup=True)
        VW(sv, sv.c)
        ry_bc = sv.c
        if CSTEP <= 12:
            return

        TW(sy, bo_y)
        TW(sv, ry_bc)
        TE(lambda e: e.matmul(tp32[:, 0:512], onesr[0:1, :], tmp2[0:1, :],
                              start=True, stop=True), s2, 1)
        VW(s2, s2.c)
        VE(lambda e: e.tensor_copy(borep[:, :], tp32[:, 0:512]), sv, 1)

        if CSTEP <= 2:
            return
        VW(sg, wf_g)
        for ch in range(4):
            raw = wfraw[ch]
            VE(lambda e, raw=raw: e.tensor_reduce(stats[:, 0:1], raw, AX, OP.add,
                                                  apply_absolute_value=True), sv, 1, dup=True)
            VW(sv, sv.c)
            VE(lambda e, ch=ch: e.tensor_scalar(
                wfs[:, ch:ch + 1], stats[:, 0:1], 1.0 / 1024.0, 1e-5, OP.mult, OP.max), sv, 1, dup=True)
            VW(sv, sv.c)
            VE(lambda e, ch=ch: e.reciprocal(stats[:, 1:2], wfs[:, ch:ch + 1]), sv, 1, dup=True)
            VW(sv, sv.c)
            VE(lambda e, raw=raw: e.tensor_scalar(
                xnt[:, 0:1024], raw, stats[:, 1:2], 1.0, OP.mult, OP.min), sv, 1)
            VE(lambda e: e.tensor_scalar(
                q16[:, :], xnt[:, 0:1024], -1.0, 1536.0, OP.max, OP.add), sv, 1)
            if ch > 0:
                VW(s2, s2.c)
            VE(lambda e: e.tensor_scalar(qtm[:, :], q16[:, :], -1536.0, None, OP.add), sv, 1)
            qv = sv.c
            TW(sv, qv)
            for c2 in range(8):
                TE(lambda e, c2=c2: e.matmul(
                    tp0[:, c2 * 128:(c2 + 1) * 128], qtm[:, c2 * 128:(c2 + 1) * 128],
                    identv, start=(c2 == 0), stop=(c2 == 7), is_transpose=True),
                    s2 if c2 == 7 else None, 1)
            tv = s2.c
            VW(s2, tv)
            dst = bass.AP(wt, wfT.offset + ch * 128,
                          [[wt.shape[1], 128], [512, 8], [1, 128]])
            VE(lambda e, dst=dst: e.tensor_copy(dst, tp0.rearrange("p (a b) -> p a b", b=128)), sv, 1)
        wf_done_v = sv.c

        TW(sv, wf_done_v)
        TE(lambda e: e.matmul(tp32[0:4, 0:128], wfs[:, 0:4], ident32[:, :],
                              start=True, stop=True, is_transpose=True), s2, 1)
        VW(s2, s2.c)
        VE(lambda e: e.tensor_copy(tmp1[0:4, 0:128], tp32[0:4, 0:128]), sv, 1)
        ws_v = sv.c
        GW(sv, ws_v)
        GE(lambda e: e.dma_start(out=wsb.ap(), in_=tmp1[0:4, 0:128]), sg, 16)
        GW(sg, sg.c)
        GE(lambda e: e.dma_start(out=tmp2[0:1, :], in_=wsb.ap()), sg, 16)
        wsrow_g = sg.c
        TW(sg, wsrow_g)
        TE(lambda e: e.matmul(tp32[:, 0:512], onesr[0:1, :], tmp2[0:1, :],
                              start=True, stop=True), s2, 1)
        VW(s2, s2.c)
        VE(lambda e: e.tensor_copy(wsrep[:, :], tp32[:, 0:512]), sv, 1)

        if CSTEP <= 3:
            return
        VW(sv, ry_bc)
        nhalf = NT // 2
        for hloc in range(NHC):
            YW(sv, sv.c)
            YE(lambda e, hloc=hloc: e.dma_start(out=ystg[:, 0:NT * HD], in_=y_sp.ap()[hloc]), sy, 16)
            VW(sy, sy.c)
            for half in range(2):
                VE(lambda e, half=half: e.tensor_scalar(
                    q16[:, 0:nhalf * 64], ystg[:, half * nhalf * 64:(half + 1) * nhalf * 64],
                    sbc[:, SC_RY:SC_RY + 1], cpool[:, CP_1536:CP_1536 + 1],
                    OP.mult, OP.add), sv, 1)
                dstq = bass.AP(big2, half * nhalf * 512 + hloc * 64,
                               [[32768, 128], [512, nhalf], [1, 64]])
                srcq = q16[:, 0:nhalf * 64].rearrange("p (a b) -> p a b", b=64)
                VE(lambda e, dstq=dstq, srcq=srcq: e.tensor_scalar(
                    dstq, srcq, -1536.0, None, OP.add), sv, 1)
        yq_v = sv.c

        if CSTEP <= 4:
            return
        GW(sv, yq_v)
        GE(lambda e: e.dma_start(
            out=ag_in.ap().rearrange("(j p) i -> p j i", p=128),
            in_=big2[:, 0:NT * 512].rearrange("p (j i) -> p j i", i=512)), sg, 16)
        GW(sg, sg.c)
        GE(lambda e: e.collective_compute(
            "AllGather", OP.bypass, replica_groups=[[0, 1], [2, 3], [4, 5], [6, 7]],
            ins=[ag_in.ap().opt()], outs=[ag_out.ap().opt()]), scc, 1)
        ag_cc = scc.c

        if CSTEP <= 5:
            return
        stg_y = [sv.c, sv.c]
        YW(scc, ag_cc)
        for r in range(2):
            for jt in range(NT):
                k = r * NT + jt
                stg = aq_stg[k % 2]
                YW(sv, stg_y[k % 2])
                YE(lambda e, r=r, jt=jt, stg=stg: e.dma_start(
                    out=stg, in_=ag_out.ap()[r, jt * 128:(jt + 1) * 128, :]), sy, 16)
                TW(sy, sy.c)
                if k >= 1:
                    TW(sv, sv.c)
                for ic in range(4):
                    TE(lambda e, ic=ic, stg=stg: e.matmul(
                        tp0[:, ic * 128:(ic + 1) * 128], stg[:, ic * 128:(ic + 1) * 128],
                        identv, start=(ic == 0), stop=(ic == 3), is_transpose=True),
                        s2 if ic == 3 else None, 1)
                VW(s2, s2.c)
                dsty = bass.AP(big1, (r * 4) * T + jt * 128,
                               [[32832, 128], [T, 4], [1, 128]])
                VE(lambda e, dsty=dsty: e.tensor_copy(
                    dsty, tp0[:, 0:512].rearrange("p (a b) -> p a b", b=128)), sv, 1)
                stg_y[k % 2] = sv.c
        yqf_v = sv.c

        if CSTEP <= 6:
            return
        xq4_y = [sv.c, sv.c]
        TW(sv, yqf_v)
        for jt in range(NT):
            par = jt % 2
            SW(sv, xq4_y[par])
            SE(lambda e, jt=jt, par=par: e.dma_start(
                out=xq4stg[par].rearrange("p (c t) -> p c t", t=128),
                in_=bass.AP(xq4_sp, jt * 128, [[T, 128], [128 * T, 8], [1, 128]])), sq, 16)
            if jt >= 2:
                YW(sv, xq4_y[par])
            YE(lambda e, jt=jt, par=par: e.dma_start(
                out=xot[par], in_=x_ocol[jt * 128:(jt + 1) * 128, :]), sy, 16)
            xydone = sy.c
            if jt >= 2:
                TW(sv, sv.c)
            for ch in range(NCH):
                TE(lambda e, ch=ch, jt=jt, par=par: e.matmul(
                    mm_b[par], yqf[ch][:, jt * 128:(jt + 1) * 128],
                    woT[:, ch * 512:(ch + 1) * 512],
                    start=(ch == 0), stop=(ch == NCH - 1)),
                    s2 if ch == NCH - 1 else None, 1)
            TW(sq, sq.c)
            for ch in range(NCH):
                TE(lambda e, ch=ch, par=par: e.matmul(
                    sc_b[par], xq4stg[par][:, ch * 128:(ch + 1) * 128],
                    wfT[:, ch * 512:(ch + 1) * 512],
                    start=(ch == 0), stop=(ch == NCH - 1)),
                    s2 if ch == NCH - 1 else None, 1)
            mm_t = s2.c
            VW(s2, mm_t)
            VW(sy, xydone)
            VE(lambda e, par=par: e.scalar_tensor_tensor(
                tmp1[:, :], mm_b[par], sbc[:, SC_AO:SC_AO + 1], borep[:, :],
                OP.mult, OP.add), sv, 1)
            VE(lambda e, par=par, jt=jt: e.scalar_tensor_tensor(
                tmp2[:, :], sc_b[par], rs4[:, jt:jt + 1], wsrep[:, :],
                OP.mult, OP.mult), sv, 1)
            VE(lambda e: e.tensor_add(tmp1[:, :], tmp1[:, :], tmp2[:, :]), sv, 1)
            if jt >= 2:
                VW(sd, 16 * (jt - 1))
            VE(lambda e, par=par: e.tensor_add(obuf[par], tmp1[:, :], xot[par]), sv, 1)
            ob_v = sv.c
            xq4_y[par] = sv.c
            YW(sv, ob_v)
            YE(lambda e, jt=jt, par=par: e.dma_start(
                out=out_d[jt * 128:(jt + 1) * 128, :], in_=obuf[par]), sd, 16)
        YW(sd, sd.c)

        if debug:
            GW(sd, sd.c)
            for nm, src in (("xn", xn_sp), ("xq8", xq8_sp), ("vaug", vaug_sp),
                            ("y", y_sp), ("xq4", xq4_sp), ("yq", ag_in)):
                if nm in dbg:
                    GE(lambda e, nm=nm, src=src: e.dma_start(
                        out=dbg[nm], in_=src.ap()), sg, 16)
            if "sca" in dbg:
                GE(lambda e: e.dma_start(out=dbg["sca"], in_=srow[0:1, :]), sg, 16)
            if "yqf" in dbg:
                GE(lambda e: e.dma_start(out=dbg["yqf"], in_=big1[:, 0:NCH * T]), sg, 16)
            if "woT" in dbg:
                GE(lambda e: e.dma_start(out=dbg["woT"], in_=woT), sg, 16)
            if "wfT" in dbg:
                GE(lambda e: e.dma_start(out=dbg["wfT"], in_=wfT), sg, 16)
            if "wsrep" in dbg:
                GE(lambda e: e.dma_start(out=dbg["wsrep"], in_=wsrep), sg, 16)
            if "rs4" in dbg:
                GE(lambda e: e.dma_start(out=dbg["rs4"], in_=rs4[:, 0:NT]), sg, 16)
            GW(sg, sg.c)

    phases = [phaseA, phaseB, phaseC][:PHASES]
    for ph in phases:
        with nc.Block() as blk:
            done = {}

            def reg(engname, attr):
                def closure(e):
                    ph(engname, e)
                    for s in sems:
                        done[id(s)] = s.c
                return closure
            blk.sync(reg("sync", None))
            blk.vector(reg("vector", None))
            blk.scalar(reg("scalar", None))
            blk.tensor(reg("tensor", None))
            blk.gpsimd(reg("gpsimd", None))
            for s in sems:
                snap[id(s)] = done[id(s)]

    stx.close()
    return nc


_NC_CACHE = {}


def _get_nc(debug=False):
    if debug not in _NC_CACHE:
        _NC_CACHE[debug] = build_nc(debug)
    return _NC_CACHE[debug]


def kernel(**inputs):
    X = np.ascontiguousarray(np.asarray(inputs["L0_attn_x"], dtype=np.float32))
    Wq = np.asarray(inputs["L0_attn_Wq"], dtype=np.float32)
    Wk = np.asarray(inputs["L0_attn_Wk"], dtype=np.float32)
    Wv = np.asarray(inputs["L0_attn_Wv"], dtype=np.float32)
    Wo = np.asarray(inputs["L0_attn_Wo"], dtype=np.float32)
    Wf = np.asarray(inputs["L0_ffn_weight"], dtype=np.float32)
    bq = np.asarray(inputs["L0_attn_bq"], dtype=np.float32)
    bk = np.asarray(inputs["L0_attn_bk"], dtype=np.float32)
    bv = np.asarray(inputs["L0_attn_bv"], dtype=np.float32)
    bo = np.asarray(inputs["L0_attn_bo"], dtype=np.float32)

    nc = _get_nc(False)
    in_maps = []
    for c in range(8):
        b, g = c // 2, c % 2
        sl = slice(g * 512, (g + 1) * 512)
        in_maps.append({
            "xb": X[b],
            "x_ocol": np.ascontiguousarray(X[b][:, sl]),
            "wq": np.ascontiguousarray(Wq[sl]),
            "wk": np.ascontiguousarray(Wk[sl]),
            "wv": np.ascontiguousarray(Wv[sl]),
            "wo": np.ascontiguousarray(Wo[sl]),
            "wf": np.ascontiguousarray(Wf[sl]),
            "bq": np.ascontiguousarray(bq[sl]),
            "bk": np.ascontiguousarray(bk[sl]),
            "bv": np.ascontiguousarray(bv[sl]),
            "bo": np.ascontiguousarray(bo[sl]),
        })
    res = run_bass_kernel_spmd(nc, in_maps, core_ids=list(range(8)))
    out = np.empty((B_, T, D), np.float32)
    for c in range(8):
        b, g = c // 2, c % 2
        out[b][:, g * 512:(g + 1) * 512] = res.results[c]["out"]
    return out


# version 15 (final)
# speedup vs baseline: 1.0385x; 1.0385x over previous
import numpy as np
import ml_dtypes

import concourse.bass as bass
import concourse.mybir as mybir
from concourse.bass_utils import run_bass_kernel_spmd
from contextlib import ExitStack

F32 = mybir.dt.float32
BF16 = mybir.dt.bfloat16
F16 = mybir.dt.float16
AX = mybir.AxisListType.X
OP = mybir.AluOpType
AF = mybir.ActivationFunctionType
bfdt = ml_dtypes.bfloat16

B_, T, D = 4, 4096, 1024
HD = 64
NHC = 8
NT = T // 128
NCH = D // 128
BIGNEG = -800.0
EPS_LN = 1e-5


NSP = T // 512
PHASES = 3
CSTEP = 99
PIPE_ATTN = True


def set_T(t):
    global T, NT, NSP
    T = t
    NT = T // 128
    NSP = T // 512


def _consts():
    ident = np.eye(128, dtype=bfdt)
    ident32 = np.eye(128, dtype=np.float32)
    A = (np.arange(128)[:, None] <= np.arange(128)[None, :]).astype(bfdt)
    p = np.arange(128)[:, None]
    q = np.arange(896)[None, :]
    Bwide = (BIGNEG * (p > (q - 384))).astype(bfdt)
    h = np.array([[1.0]], dtype=np.float64)
    while h.shape[0] < D:
        h = np.block([[h, h], [h, -h]])
    H = (h / 32.0).astype(bfdt)
    ones_row = np.ones((1, 128), dtype=np.float32)
    selAB = np.zeros((128, 2), dtype=bfdt)
    selAB[:64, 0] = 1
    selAB[64:, 1] = 1
    return ident, ident32, A, Bwide, H, ones_row, selAB


class Sem:
    __slots__ = ("h", "c")

    def __init__(self, h):
        self.h = h
        self.c = 0


def build_nc(debug=False):
    nc = bass.Bass("TRN2", target_bir_lowering=False, detect_race_conditions=False)

    xb = nc.dram_tensor("xb", [T, D], F32, kind="ExternalInput").ap()
    x_ocol = nc.dram_tensor("x_ocol", [T, 512], F32, kind="ExternalInput").ap()
    wq_d = nc.dram_tensor("wq", [512, D], F32, kind="ExternalInput").ap()
    wk_d = nc.dram_tensor("wk", [512, D], F32, kind="ExternalInput").ap()
    wv_d = nc.dram_tensor("wv", [512, D], F32, kind="ExternalInput").ap()
    wo_d = nc.dram_tensor("wo", [512, D], F32, kind="ExternalInput").ap()
    wf_d = nc.dram_tensor("wf", [512, D], F32, kind="ExternalInput").ap()
    bq_d = nc.dram_tensor("bq", [512], F32, kind="ExternalInput").ap()
    bk_d = nc.dram_tensor("bk", [512], F32, kind="ExternalInput").ap()
    bv_d = nc.dram_tensor("bv", [512], F32, kind="ExternalInput").ap()
    bo_d = nc.dram_tensor("bo", [512], F32, kind="ExternalInput").ap()
    out_d = nc.dram_tensor("out", [T, 512], F32, kind="ExternalOutput").ap()

    dbg = {}
    if debug:
        for nm, sh in (("xn", [T, D]), ("xq8", [NCH, 128, T]), ("vaug", [NHC, 128, NT * 129]),
                       ("y", [NHC, 128, NT * HD]), ("xq4", [NCH, 128, T]), ("yq", [T, 512]),
                       ("sca", [1, 16]), ("yqf", [128, NCH * T]), ("woT", [128, 4096]),
                       ("lnst", [128, 11 * NT]),
                       ("wfT", [128, 4096]), ("wsrep", [128, 512]), ("rs4", [128, NT])):
            dbg[nm] = nc.dram_tensor("dbg_" + nm, sh, F32, kind="ExternalOutput").ap()

    xn_sp = nc.dram_tensor("xn_sp", [T, D], F32)
    xq8_sp = nc.dram_tensor("xq8_sp", [NCH, 128, T], BF16)
    xq4_sp = nc.dram_tensor("xq4_sp", [NCH, 128, T], BF16)
    vaug_sp = nc.dram_tensor("vaug_sp", [NHC, 128, NT * 129], BF16)
    y_sp = nc.dram_tensor("y_sp", [NHC, 128, NT * HD], F32)
    cc1_in = nc.dram_tensor("cc1_in", [5], F32)
    cc1_out = nc.dram_tensor("cc1_out", [5], F32)
    cc2_in = nc.dram_tensor("cc2_in", [1], F32)
    cc2_out = nc.dram_tensor("cc2_out", [1], F32)
    wsb = nc.dram_tensor("wsb", [512], F32)
    ag_in = nc.dram_tensor("ag_in", [T, 512], BF16)
    ag_out = nc.dram_tensor("ag_out", [2, T, 512], BF16)

    ident_np, ident32_np, A_np, Bw_np, H_np, ones_np, selAB_np = _consts()
    ident_d = nc.inline_tensor(ident_np, "c_ident").ap()
    ident32_d = nc.inline_tensor(ident32_np, "c_ident32").ap()
    A_d = nc.inline_tensor(A_np, "c_A").ap()
    Bw_d = nc.inline_tensor(Bw_np, "c_Bw").ap()
    H_d = nc.inline_tensor(H_np, "c_H").ap()
    ones_d = nc.inline_tensor(ones_np, "c_ones").ap()
    selAB_d = nc.inline_tensor(selAB_np, "c_selAB").ap()

    stx = ExitStack()
    sbuf = lambda n_, s, d: stx.enter_context(nc.sbuf_tensor(n_, s, d))

    big1 = sbuf("big1", [128, 32832], BF16)
    big2 = sbuf("big2", [128, 32768], BF16)
    wt = sbuf("wt", [128, 4 * 4096 + 8192], BF16)
    xst = sbuf("xst", [128, 2 * D], F32)
    xnt = sbuf("xnt", [128, 2 * D], F32)
    q16 = sbuf("q16", [128, D], F16)
    qtm = sbuf("qtm", [128, D], BF16)
    vstage = sbuf("vstage", [128, 2 * 8 * 129], BF16)
    stats = sbuf("stats", [128, 32], F32)
    lnst = sbuf("lnst", [128, 11 * NT], F32)
    scal = sbuf("scal", [128, 8], F32)
    sbc = sbuf("sbc", [128, 12], F32)
    srow = sbuf("srow", [1, 16], F32)
    mrow = sbuf("mrow", [1, 16], F32)
    mbc = sbuf("mbc", [128, 8], F32)
    cpool = sbuf("cpool", [128, 8], F32)
    consts = sbuf("consts", [128, 128 + 128 + 896], BF16)
    ident32 = sbuf("ident32", [128, 128], F32)
    onesr = sbuf("onesr", [1, 128], F32)
    selAB = sbuf("selAB", [128, 2], BF16)
    brow = sbuf("brow", [128, 8], F32)
    bvrep = sbuf("bvrep", [128, 512], F32)
    wfs = sbuf("wfs", [128, 4], F32)
    rs4 = sbuf("rs4", [128, NT], F32)
    tmp1 = xst[:, 0:512]
    tmp2 = xst[:, 512:1024]
    qn2 = sbuf("qn2", [128, 6], F32)

    xq8f = [big1[:, ch * T:(ch + 1) * T] for ch in range(NCH)]
    stKs = [big1[:, 0 * T:1 * T], big1[:, 3 * T:4 * T]]
    stQHs = [big1[:, 1 * T:2 * T], big1[:, 4 * T:5 * T]]
    stQLs = [big1[:, 2 * T:3 * T], big1[:, 5 * T:6 * T]]
    vaug0 = big1[:, 6 * T:6 * T + NT * 129]
    yh0 = 6 * T + NT * 129 + (NT * 129) % 2
    yhead = big1[:, yh0:yh0 + NT * 128].bitcast(F32)
    yqf = [big1[:, ch * T:(ch + 1) * T] for ch in range(NCH)]

    wraw = [big2[:, i * 2048:(i + 1) * 2048].bitcast(F32) for i in range(16)]
    pairQH = big2[:, 0 * T:1 * T]
    pairQL = big2[:, 1 * T:2 * T]
    pairKH = big2[:, 2 * T:3 * T]
    pairKL = big2[:, 3 * T:4 * T]
    xq8stg = [big2[:, 4 * T + i * T:4 * T + (i + 1) * T] for i in range(2)]
    RINGN = 4
    r0 = 6 * T
    ringPh = [big2[:, r0 + i * 512:r0 + (i + 1) * 512] for i in range(RINGN)]
    r1 = r0 + RINGN * 512
    ringPl = [big2[:, r1 + i * 512:r1 + (i + 1) * 512] for i in range(RINGN)]
    r2 = r1 + RINGN * 512
    pf32 = [big2[:, r2:r2 + 2048].bitcast(F32),
            big2[:, r2 + 2048:r2 + 4096].bitcast(F32)]
    sqq = big2[:, 4 * T:5 * T]
    vaug1 = big2[:, 0:NT * 129]
    vaugs = [vaug0, vaug1]
    ha = 4 * T if T < 4096 else 0
    hi_tm = big2[:, ha + 0:ha + 1024]
    lo_tm = big2[:, ha + 1024:ha + 2048]
    hif = big2[:, ha + 2048:ha + 3072]
    lof = big2[:, ha + 3072:ha + 4096]
    yq_tm = big2
    wfraw = [big2[:, NT * 512 + i * 2048:NT * 512 + (i + 1) * 2048].bitcast(F32) for i in range(4)]
    xq4stg = [big2[:, NT * 512 + i * 1024:NT * 512 + (i + 1) * 1024] for i in range(2)]
    ystg = big2[:, NT * 512 + 8192:NT * 512 + 8192 + 4096].bitcast(F32)
    aq_stg = [big2[:, NT * 512 + 12288 + i * 512:NT * 512 + 12288 + (i + 1) * 512] for i in range(2)]

    wqT = wt[:, 0 * 4096:1 * 4096]
    wkT = wt[:, 1 * 4096:2 * 4096]
    wvT = wt[:, 2 * 4096:3 * 4096]
    woT = wt[:, 3 * 4096:4 * 4096]
    hbuf = wt[:, 4 * 4096:4 * 4096 + 8192]
    wfT = wt[:, 0 * 4096:1 * 4096]
    wsrep = wt[:, 4 * 4096:4 * 4096 + 1024].bitcast(F32)
    borep = wt[:, 4 * 4096 + 1024:4 * 4096 + 2048].bitcast(F32)
    xot = [wt[:, 4 * 4096 + 2048 + i * 1024:4 * 4096 + 2048 + (i + 1) * 1024].bitcast(F32)
           for i in range(2)]
    obuf = [wt[:, 4 * 4096 + 4096 + i * 1024:4 * 4096 + 4096 + (i + 1) * 1024].bitcast(F32)
            for i in range(2)]

    identv = consts[:, 0:128]
    Av = consts[:, 128:256]
    Bv = [consts[:, 256 + 384 - 128 * c: 256 + 896 - 128 * c] for c in range(4)]

    MC_XN, MC_WQ, MC_WK, MC_WV, MC_WO, MC_Y = 0, 1, 2, 3, 4, 5
    SC_RX, SC_AQ, SC_AK, SC_AV, SC_RWQ, SC_RWK, SC_RWV, SC_RWO, SC_RY, SC_AO = range(10)
    CP_125, CP_1536, CP_N1536, CP_1 = range(4)

    pmm = stx.enter_context(nc.psum_tensor("pmm", [128, 8, 512], F32))
    sc_b = [pmm[:, 0, :], pmm[:, 1, :]]
    pvb = [pmm[:, 2 + i, :] for i in range(4)]
    mm_b = [pmm[:, 6, :], pmm[:, 7, :]]
    tp0 = pmm[:, 2, :].bitcast(BF16)
    tp1 = pmm[:, 3, :].bitcast(BF16)
    tp2 = pmm[:, 4, :].bitcast(BF16)
    tp32 = pmm[:, 5, :]
    tpB = pmm[:, 7, :]

    sems = []

    def mksem(name):
        s = Sem(stx.enter_context(nc.semaphore(name)))
        sems.append(s)
        return s

    sy = mksem("sy")
    sg = mksem("sg")
    sq = mksem("sq")
    sv = mksem("sv")
    ss = mksem("ss")
    s2 = mksem("s2")
    scc = mksem("scc")
    sd = mksem("sd")

    snap = {id(s): 0 for s in sems}

    def mkwalk(cur, eng):
        for s in sems:
            s.c = snap[id(s)]

        def mk(name):
            def em(fn, inc=None, n=1, dup=False):
                reps = 2 if dup else 1
                for _ in range(reps):
                    if cur == name:
                        ins = fn(eng)
                        if inc is not None:
                            ins.then_inc(inc.h, n)
                    if inc is not None:
                        inc.c += n

            def wt_(sem, val=None):
                if cur == name:
                    eng.wait_ge(sem.h, sem.c if val is None else val)
            return em, wt_
        return mk

    def phaseA(cur, eng):
        mk = mkwalk(cur, eng)
        VE, VW = mk("vector")
        TE, TW = mk("tensor")
        SE, SW = mk("scalar")
        YE, YW = mk("sync")
        GE, GW = mk("gpsimd")

        YE(lambda e: e.dma_start(out=identv, in_=ident_d), sy, 16)
        YE(lambda e: e.dma_start(out=Av, in_=A_d), sy, 16)
        YE(lambda e: e.dma_start(out=consts[:, 256:256 + 896], in_=Bw_d), sy, 16)
        YE(lambda e: e.dma_start(out=ident32[:, :], in_=ident32_d), sy, 16)
        YE(lambda e: e.dma_start(out=onesr[:, :], in_=ones_d), sy, 16)
        YE(lambda e: e.dma_start(out=selAB[:, :], in_=selAB_d), sy, 16)
        YE(lambda e: e.dma_start(out=hbuf.rearrange("p (c j) -> p c j", j=1024),
                                 in_=H_d.rearrange("(c p) j -> p c j", p=128)), sy, 16)
        cst_y = sy.c

        for bd, col in ((bq_d, 0), (bk_d, 4)):
            GE(lambda e, bd=bd, col=col: e.dma_start(
                out=brow[:, col:col + 4], in_=bd.rearrange("(c p) -> p c", p=128),
                allow_slow_non_contiguous=True), sg, 16)
        for i, wd in enumerate((wq_d, wk_d, wv_d, wo_d)):
            for ch in range(4):
                GE(lambda e, wd=wd, k=i * 4 + ch, ch=ch: e.dma_start(
                    out=wraw[k], in_=wd[ch * 128:(ch + 1) * 128, :]), sg, 16)
        wld_g = sg.c

        for col, val in ((CP_125, 0.125), (CP_1536, 1536.0), (CP_N1536, -1536.0), (CP_1, 1.0)):
            VE(lambda e, col=col, val=val: e.memset(cpool[:, col:col + 1], val), sv, 1)
        VE(lambda e: e.memset(srow[0:1, :], 0.0), sv, 1, dup=True)
        VW(sv, sv.c)
        VE(lambda e: e.memset(mrow[0:1, :], 0.0), sv, 1, dup=True)
        VW(sv, sv.c)

        LS_SUM, LS_MAX, LS_MIN, LS_S1, LS_S2, LS_RS, LS_NM, LS_XM = (i * NT for i in range(8))
        p1_free = [None] * NT
        p1_free_s = [None] * NT
        for j in range(NT):
            xt = xst[:, (j % 2) * D:((j % 2) + 1) * D]
            if j >= 2:
                YW(sv, p1_free[j - 2])
                YW(ss, p1_free_s[j - 2])
            YE(lambda e, j=j, xt=xt: e.dma_start(out=xt, in_=xb[j * 128:(j + 1) * 128, :]), sy, 16)
            VW(sy, cst_y + 16 * (j + 1))
            VE(lambda e, xt=xt, j=j: e.tensor_reduce(lnst[:, LS_SUM + j:LS_SUM + j + 1], xt, AX, OP.add), sv, 1, dup=True)
            VE(lambda e, xt=xt, j=j: e.tensor_reduce(lnst[:, LS_MAX + j:LS_MAX + j + 1], xt, AX, OP.max), sv, 1, dup=True)
            VE(lambda e, xt=xt, j=j: e.tensor_reduce(lnst[:, LS_MIN + j:LS_MIN + j + 1], xt, AX, OP.min), sv, 1, dup=True)
            SW(sy, cst_y + 16 * (j + 1))
            if j >= 1:
                SW(sv, psred[j - 1])
            SE(lambda e, xt=xt: e.activation(mm_b[0], xt[:, 0:512], AF.Square, bias=0.0, scale=1.0), ss, 1, dup=True)
            SE(lambda e, xt=xt: e.activation(mm_b[1], xt[:, 512:1024], AF.Square, bias=0.0, scale=1.0), ss, 1, dup=True)
            p1_free_s[j] = ss.c
            VW(ss, ss.c)
            VE(lambda e, j=j: e.tensor_reduce(lnst[:, LS_S1 + j:LS_S1 + j + 1], mm_b[0], AX, OP.add), sv, 1, dup=True)
            VE(lambda e, j=j: e.tensor_reduce(lnst[:, LS_S2 + j:LS_S2 + j + 1], mm_b[1], AX, OP.add), sv, 1, dup=True)
            if j == 0:
                psred = [None] * NT
            psred[j] = sv.c
            p1_free[j] = sv.c

        LS_T1, LS_T2, LS_T3 = 8 * NT, 9 * NT, 10 * NT
        L = lambda o: lnst[:, o:o + NT]
        VE(lambda e: e.tensor_scalar(L(LS_T1), L(LS_SUM), 1.0 / 1024.0, None, OP.mult), sv, 1, dup=True)
        VW(sv, sv.c)
        VE(lambda e: e.tensor_add(L(LS_T2), L(LS_S1), L(LS_S2)), sv, 1, dup=True)
        VW(sv, sv.c)
        VE(lambda e: e.tensor_scalar(L(LS_T2), L(LS_T2), 0.0, None, OP.add) if False else
           e.tensor_scalar(L(LS_S1), L(LS_T2), 1.0 / 1024.0, None, OP.mult), sv, 1, dup=True)
        VW(sv, sv.c)
        VE(lambda e: e.tensor_tensor(L(LS_S2), L(LS_T1), L(LS_T1), OP.mult), sv, 1, dup=True)
        VW(sv, sv.c)
        VE(lambda e: e.tensor_sub(L(LS_T2), L(LS_S1), L(LS_S2)), sv, 1, dup=True)
        VW(sv, sv.c)
        VE(lambda e: e.tensor_scalar(L(LS_T3), L(LS_T2), EPS_LN, None, OP.add), sv, 1, dup=True)
        ln_v = sv.c
        SW(sv, ln_v)
        SE(lambda e: e.sqrt(L(LS_S1), L(LS_T3)), ss, 1, dup=True)
        VW(ss, ss.c)
        VE(lambda e: e.reciprocal(L(LS_RS), L(LS_S1)), sv, 1, dup=True)
        VW(sv, sv.c)
        VE(lambda e: e.tensor_tensor(L(LS_T2), L(LS_T1), L(LS_RS), OP.mult), sv, 1, dup=True)
        VW(sv, sv.c)
        VE(lambda e: e.tensor_scalar(L(LS_NM), L(LS_T2), -1.0, None, OP.mult), sv, 1, dup=True)
        VW(sv, sv.c)
        VE(lambda e: e.tensor_sub(L(LS_T2), L(LS_MAX), L(LS_T1)), sv, 1, dup=True)
        VW(sv, sv.c)
        VE(lambda e: e.tensor_sub(L(LS_T3), L(LS_T1), L(LS_MIN)), sv, 1, dup=True)
        VW(sv, sv.c)
        VE(lambda e: e.tensor_tensor(L(LS_XM), L(LS_T2), L(LS_T3), OP.max), sv, 1, dup=True)
        VW(sv, sv.c)
        VE(lambda e: e.tensor_tensor(L(LS_T2), L(LS_XM), L(LS_RS), OP.mult), sv, 1, dup=True)
        VW(sv, sv.c)
        VE(lambda e: e.tensor_reduce(scal[:, MC_XN:MC_XN + 1], L(LS_T2),
                                     AX, OP.max), sv, 1, dup=True)
        VW(sv, sv.c)

        VW(sg, wld_g)
        for i in range(4):
            for ch in range(4):
                view = wraw[i * 4 + ch]
                if ch == 0:
                    VE(lambda e, view=view, i=i: e.tensor_reduce(
                        scal[:, 1 + i:2 + i], view, AX, OP.max, apply_absolute_value=True), sv, 1, dup=True)
                    VW(sv, sv.c)
                else:
                    VE(lambda e, view=view: e.tensor_reduce(
                        stats[:, 0:1], view, AX, OP.max, apply_absolute_value=True), sv, 1, dup=True)
                    VW(sv, sv.c)
                    VE(lambda e, i=i: e.tensor_tensor(
                        scal[:, 1 + i:2 + i], scal[:, 1 + i:2 + i], stats[:, 0:1], OP.max), sv, 1, dup=True)
                    VW(sv, sv.c)
        VE(lambda e: e.tensor_copy(tmp1[:, 0:5], scal[:, 0:5]), sv, 1)
        maxes_v = sv.c

        TW(sv, maxes_v)
        TW(sy, cst_y)
        TE(lambda e: e.matmul(tp32[0:5, 0:128], tmp1[:, 0:5], ident32[:, :],
                              start=True, stop=True, is_transpose=True), s2, 1)
        VW(s2, s2.c)
        VE(lambda e: e.tensor_reduce(stats[0:5, 0:1], tp32[0:5, 0:128], AX, OP.max), sv, 1, dup=True)
        VW(sv, sv.c)
        red_v = sv.c
        GW(sv, red_v)
        GE(lambda e: e.dma_start(out=cc1_in.ap()[0:5], in_=stats[0:5, 0:1]), sg, 16)
        GW(sg, sg.c)
        GE(lambda e: e.collective_compute(
            "AllReduce", OP.max, replica_groups=[list(range(8))],
            ins=[cc1_in.ap().opt()], outs=[cc1_out.ap().opt()]), scc, 1)
        GW(scc, scc.c)
        GE(lambda e: e.dma_start(out=srow[0:1, 0:5], in_=cc1_out.ap()[0:5]), sg, 16)
        cc1_g = sg.c

        VW(sg, cc1_g)
        for i in range(5):
            VE(lambda e, i=i: e.tensor_scalar(
                srow[0:1, 8 + i:9 + i], srow[0:1, i:i + 1], 1.0 / 127.0, 1e-8,
                OP.mult, OP.max), sv, 1, dup=True)
            VW(sv, sv.c)
        VE(lambda e: e.reciprocal(mrow[0:1, 0:5], srow[0:1, 8:13]), sv, 1, dup=True)
        VW(sv, sv.c)
        VE(lambda e: e.tensor_copy(mrow[0:1, 8:9], mrow[0:1, 0:1]), sv, 1, dup=True)
        VW(sv, sv.c)
        for i, scol in ((0, 9), (1, 10), (2, 11)):
            VE(lambda e, i=i, scol=scol: e.tensor_single_scalar(
                mrow[0:1, scol:scol + 1], srow[0:1, 9 + i:10 + i], srow[0:1, 8:9], OP.mult), sv, 1, dup=True)
            VW(sv, sv.c)
        VE(lambda e: e.tensor_copy(mrow[0:1, 12:16], mrow[0:1, 1:5]), sv, 1, dup=True)
        VW(sv, sv.c)
        scl_v = sv.c
        TW(sv, scl_v)
        TE(lambda e: e.matmul(tp32[:, 0:8], onesr[0:1, :], mrow[0:1, 8:16],
                              start=True, stop=True), s2, 1)
        VW(s2, s2.c)
        VE(lambda e: e.tensor_copy(sbc[:, 0:8], tp32[:, 0:8]), sv, 1, dup=True)
        VW(sv, sv.c)
        sbc_v = sv.c

        YW(sv, sbc_v)
        YE(lambda e: e.dma_start(out=tmp2[0:1, :], in_=bv_d), sy, 16)
        TW(sy, sy.c)
        TW(sv, sbc_v)
        TE(lambda e: e.matmul(tp32[:, 0:512], onesr[0:1, :], tmp2[0:1, :],
                              start=True, stop=True), s2, 1)
        VW(s2, s2.c)
        VE(lambda e: e.tensor_copy(bvrep[:, :], tp32[:, 0:512]), sv, 1)

        for i, (rw_col, wT) in enumerate(((SC_RWQ, wqT), (SC_RWK, wkT),
                                          (SC_RWV, wvT), (SC_RWO, woT))):
            for ch in range(4):
                raw = wraw[i * 4 + ch]
                if not (i == 0 and ch == 0):
                    VW(s2, s2.c)
                VE(lambda e, raw=raw, rw_col=rw_col: e.tensor_scalar(
                    q16[:, :], raw, sbc[:, rw_col:rw_col + 1],
                    cpool[:, CP_1536:CP_1536 + 1], OP.mult, OP.add), sv, 1)
                VE(lambda e: e.tensor_scalar(qtm[:, :], q16[:, :], -1536.0, None, OP.add), sv, 1)
                qv = sv.c
                TW(sv, qv)
                for c2 in range(8):
                    TE(lambda e, c2=c2: e.matmul(
                        tp0[:, c2 * 128:(c2 + 1) * 128], qtm[:, c2 * 128:(c2 + 1) * 128],
                        identv, start=(c2 == 0), stop=(c2 == 7), is_transpose=True),
                        s2 if c2 == 7 else None, 1)
                tv = s2.c
                VW(s2, tv)
                dst = bass.AP(wt, wT.offset + ch * 128,
                              [[wt.shape[1], 128], [512, 8], [1, 128]])
                VE(lambda e, dst=dst: e.tensor_copy(
                    dst, tp0.rearrange("p (a b) -> p a b", b=128)), sv, 1)
        wq_done_v = sv.c

        hi2 = [big2[:, ha + p * 6144 + 0:ha + p * 6144 + 1024] for p in range(2)]
        lo2 = [big2[:, ha + p * 6144 + 1024:ha + p * 6144 + 2048] for p in range(2)]
        hf2 = [big2[:, ha + p * 6144 + 2048:ha + p * 6144 + 3072] for p in range(2)]
        lf2 = [big2[:, ha + p * 6144 + 3072:ha + p * 6144 + 4096] for p in range(2)]
        qm8 = [big2[:, ha + p * 6144 + 4096:ha + p * 6144 + 5120] for p in range(2)]
        q68 = [big2[:, ha + p * 6144 + 5120:ha + p * 6144 + 6144].bitcast(F16) for p in range(2)]
        q64 = [hf2[p].bitcast(F16) for p in range(2)]
        tpv = [[tp0, tp1, tp2],
               [pmm[:, 5, :].bitcast(BF16), pmm[:, 0, :].bitcast(BF16), pmm[:, 1, :].bitcast(BF16)]]
        b6t = pmm[:, 6, :].bitcast(BF16)
        bsv, bss, bs2, bsq, bsy = sv.c, ss.c, s2.c, sq.c, sy.c
        ldy, normv, lov, shi, t1t, e1v, t2t, v2r, sdq, t3t, e2v, spq, dbgq = ({} for _ in range(13))

        def g(d, k, base):
            return d.get(k, base)
        xq4_spill_q = spq
        for j in range(NT + 2):
            p = j % 2
            p1 = (j - 1) % 2
            if j < 2:
                YW(sv, wq_done_v)
                YE(lambda e, j=j: e.dma_start(
                    out=xnt[:, (j % 2) * D:((j % 2) + 1) * D],
                    in_=xb[j * 128:(j + 1) * 128, :]), sy, 16)
                ldy[j] = sy.c
            if j < NT:
                xnb = xnt[:, p * D:(p + 1) * D]
                VW(sy, ldy[j])
                VE(lambda e, xnb=xnb, j=j: e.tensor_scalar(
                    xnb, xnb, lnst[:, LS_RS + j:LS_RS + j + 1], lnst[:, LS_NM + j:LS_NM + j + 1],
                    OP.mult, OP.add), sv, 1)
                normv[j] = sv.c
                SW(sv, normv[j])
                SW(s2, g(t1t, j - 2, bs2))
                SE(lambda e, xnb=xnb, p=p: e.activation(
                    q68[p][:, :], xnb, AF.Identity,
                    bias=cpool[:, CP_1536:CP_1536 + 1], scale=sbc[:, SC_RX:SC_RX + 1]), ss, 1)
                SE(lambda e, p=p: e.activation(
                    qm8[p][:, :], q68[p][:, :], AF.Identity,
                    bias=cpool[:, CP_N1536:CP_N1536 + 1], scale=cpool[:, CP_1:CP_1 + 1]), ss, 1)
                SE(lambda e, xnb=xnb, p=p: e.activation(
                    hi2[p][:, :], xnb, AF.Identity,
                    bias=0.0, scale=1.0), ss, 1)
                shi[j] = ss.c
                if debug:
                    SE(lambda e, j=j, xnb=xnb: e.dma_start(
                        out=xn_sp.ap()[j * 128:(j + 1) * 128, :], in_=xnb), sq, 16)
                    dbgq[j] = sq.c
                VW(ss, shi[j])
                VW(sq, g(spq, j - 2, bsq))
                VE(lambda e, xnb=xnb, p=p: e.scalar_tensor_tensor(
                    lo2[p], xnb, 1.0, hi2[p], OP.mult, OP.subtract), sv, 1)
                lov[j] = sv.c
                if j + 2 < NT:
                    jn = j + 2
                    YW(sv, lov[j])
                    YW(ss, shi[j])
                    if debug:
                        YW(sq, dbgq[j])
                    YE(lambda e, jn=jn: e.dma_start(
                        out=xnt[:, (jn % 2) * D:((jn % 2) + 1) * D],
                        in_=xb[jn * 128:(jn + 1) * 128, :]), sy, 16)
                    ldy[jn] = sy.c
                TW(ss, shi[j])
                TW(sv, lov[j])
                TW(sv, g(e1v, j - 2, bsv))
                for bi, srct in enumerate((qm8[p], hi2[p], lo2[p])):
                    for c2 in range(8):
                        TE(lambda e, bi=bi, c2=c2, srct=srct, p=p: e.matmul(
                            tpv[p][bi][:, c2 * 128:(c2 + 1) * 128], srct[:, c2 * 128:(c2 + 1) * 128],
                            identv, start=(c2 == 0), stop=(c2 == 7), is_transpose=True),
                            s2 if (bi == 2 and c2 == 7) else None, 1)
                t1t[j] = s2.c
            if j - 2 >= 0:
                j2 = j - 2
                so = p * 8
                VW(s2, t2t[j2])
                VE(lambda e, so=so: e.tensor_reduce(stats[:, so:so + 1], mm_b[0], AX, OP.max,
                                                    apply_absolute_value=True), sv, 1, dup=True)
                VW(sv, sv.c)
                VE(lambda e, so=so: e.tensor_reduce(stats[:, so + 1:so + 2], mm_b[1], AX, OP.max,
                                                    apply_absolute_value=True), sv, 1, dup=True)
                VW(sv, sv.c)
                VE(lambda e, so=so: e.tensor_tensor(stats[:, so + 2:so + 3], stats[:, so:so + 1],
                                                    stats[:, so + 1:so + 2], OP.max), sv, 1, dup=True)
                VW(sv, sv.c)
                VE(lambda e, so=so: e.tensor_scalar(stats[:, so + 3:so + 4], stats[:, so + 2:so + 3],
                                                    1e-5, 1.0 / 7.0, OP.max, OP.mult), sv, 1, dup=True)
                VW(sv, sv.c)
                VE(lambda e, so=so, j2=j2: e.tensor_copy(rs4[:, j2:j2 + 1], stats[:, so + 3:so + 4]),
                   sv, 1, dup=True)
                VW(sv, sv.c)
                VE(lambda e, so=so: e.reciprocal(stats[:, so + 4:so + 5], stats[:, so + 3:so + 4]),
                   sv, 1, dup=True)
                VW(sv, sv.c)
                v2r[j2] = sv.c
                SW(sv, v2r[j2])
                for jh in range(2):
                    SE(lambda e, jh=jh, so=so, p=p: e.activation(
                        q64[p][:, jh * 512:(jh + 1) * 512], mm_b[jh], AF.Identity,
                        bias=cpool[:, CP_1536:CP_1536 + 1], scale=stats[:, so + 4:so + 5]), ss, 1)
                SE(lambda e, p=p: e.activation(
                    lf2[p][:, :], q64[p][:, :], AF.Identity,
                    bias=cpool[:, CP_N1536:CP_N1536 + 1], scale=cpool[:, CP_1:CP_1 + 1]), ss, 1)
                sdq[j2] = ss.c
                TW(ss, sdq[j2])
                for c2 in range(8):
                    TE(lambda e, c2=c2, p=p: e.matmul(
                        b6t[:, c2 * 128:(c2 + 1) * 128], lf2[p][:, c2 * 128:(c2 + 1) * 128],
                        identv, start=(c2 == 0), stop=(c2 == 7), is_transpose=True),
                        s2 if c2 == 7 else None, 1)
                t3t[j2] = s2.c
            if j - 1 >= 0 and j - 1 < NT:
                j1 = j - 1
                VW(s2, t1t[j1])
                VW(s2, g(t3t, j1 - 2, bs2))
                dst8 = bass.AP(big1, j1 * 128, [[32832, 128], [T, 8], [1, 128]])
                VE(lambda e, dst8=dst8, p1=p1: e.tensor_copy(
                    dst8, tpv[p1][0].rearrange("p (a b) -> p a b", b=128)), sv, 1)
                VE(lambda e, p1=p1: e.tensor_copy(hf2[p1], tpv[p1][1]), sv, 1)
                VE(lambda e, p1=p1: e.tensor_copy(lf2[p1], tpv[p1][2]), sv, 1)
                e1v[j1] = sv.c
            if j - 2 >= 0:
                j2 = j - 2
                VW(s2, t3t[j2])
                if j < NT:
                    VW(s2, t1t[j])
                VE(lambda e, p=p: e.tensor_copy(lo2[p], b6t[:, :]), sv, 1)
                e2v[j2] = sv.c
            if j - 1 >= 0 and j - 1 < NT:
                j1 = j - 1
                TW(sv, e1v[j1])
                TW(sv, g(e2v, j1 - 1, bsv))
                TW(ss, g(sdq, j1 - 1, bss))
                n_mm = 0
                for jh in range(2):
                    for c2 in range(8):
                        for srct in (hf2[p1], lf2[p1]):
                            last = (n_mm == 31)
                            TE(lambda e, jh=jh, c2=c2, srct=srct, first=(n_mm % 16 == 0), last=last:
                               e.matmul(mm_b[jh], srct[:, c2 * 128:(c2 + 1) * 128],
                                        hbuf[:, c2 * 1024 + jh * 512: c2 * 1024 + jh * 512 + 512],
                                        start=first, stop=(n_mm % 16 == 15)),
                               s2 if last else None, 1)
                            n_mm += 1
                t2t[j1] = s2.c
            if j - 2 >= 0:
                j2 = j - 2
                SW(sv, e2v[j2])
                SE(lambda e, j2=j2, p=p: e.dma_start(
                    out=bass.AP(xq4_sp, j2 * 128, [[T, 128], [128 * T, 8], [1, 128]]),
                    in_=lo2[p].rearrange("p (a b) -> p a b", b=128)), sq, 16)
                spq[j2] = sq.c

        SW(sv, sv.c)
        for ch in range(NCH):
            SE(lambda e, ch=ch: e.dma_start(out=xq8_sp.ap()[ch], in_=xq8f[ch]), sq, 16)

        TW(ss, ss.c)
        bank_free = [sv.c, sv.c]
        stg_free = [sq.c, sq.c]
        for j in range(NT):
            bank = mm_b[j % 2]
            TW(sv, bank_free[j % 2])
            for ch in range(NCH):
                TE(lambda e, bank=bank, ch=ch, j=j: e.matmul(
                    bank, xq8f[ch][:, j * 128:(j + 1) * 128], wvT[:, ch * 512:(ch + 1) * 512],
                    start=(ch == 0), stop=(ch == NCH - 1)),
                    s2 if ch == NCH - 1 else None, 1)
            vt = s2.c
            VW(s2, vt)
            VE(lambda e, bank=bank: e.scalar_tensor_tensor(
                tmp1[:, :], bank, sbc[:, SC_AV:SC_AV + 1], bvrep[:, :], OP.mult, OP.add), sv, 1)
            bank_free[j % 2] = sv.c
            VW(sq, stg_free[j % 2])
            so = (j % 2) * 1032
            hidst = bass.AP(vstage, so, [[2 * 1032, 128], [129, 8], [1, 64]])
            hisrc = tmp1.rearrange("p (h c) -> p h c", c=64)
            VE(lambda e, hidst=hidst, hisrc=hisrc: e.tensor_copy(hidst, hisrc), sv, 1)
            lodst = bass.AP(vstage, so + 65, [[2 * 1032, 128], [129, 8], [1, 64]])
            VE(lambda e, lodst=lodst, hisrc=hisrc, hidst=hidst: e.scalar_tensor_tensor(
                lodst, hisrc, 1.0, hidst, OP.mult, OP.subtract), sv, 1)
            onedst = bass.AP(vstage, so + 64, [[2 * 1032, 128], [129, 8], [1, 1]])
            VE(lambda e, onedst=onedst: e.memset(onedst, 1.0), sv, 1)
            spl_v = sv.c
            SW(sv, spl_v)
            SE(lambda e, j=j, so=so: e.dma_start(
                out=bass.AP(vaug_sp, j * 129, [[NT * 129, 128], [128 * NT * 129, 8], [1, 129]]),
                in_=bass.AP(vstage, so, [[2 * 1032, 128], [129, 8], [1, 129]])), sq, 16)
            stg_free[j % 2] = sq.c
        if debug:
            GW(sv, sv.c)
            GW(sq, sq.c)
            GE(lambda e: e.dma_start(out=dbg["lnst"], in_=lnst[:, :]), sg, 16)
            GE(lambda e: e.dma_start(out=dbg["xn"], in_=xn_sp.ap()), sg, 16)
            GW(sg, sg.c)

    def phaseB(cur, eng):
        mk = mkwalk(cur, eng)
        VE, VW = mk("vector")
        TE, TW = mk("tensor")
        SE, SW = mk("scalar")
        YE, YW = mk("sync")
        GE, GW = mk("gpsimd")

        stacks_free_t = s2.c
        yhead_free_y = sy.c
        va_free = [s2.c, s2.c]
        va_y = [sy.c, sy.c]
        pair_sq_ss = ss.c

        YW(sq, sq.c)
        GW(sq, sq.c)

        for hp in range(4):
            for proj in range(2):
                wT = wqT if proj == 0 else wkT
                acol = SC_AQ if proj == 0 else SC_AK
                bcol = 0 + hp if proj == 0 else 4 + hp
                dst_hi = pairQH if proj == 0 else pairKH
                dst_lo = pairQL if proj == 0 else pairKL
                stg_t = [s2.c, s2.c]
                for ch in range(NCH):
                    stg = xq8stg[ch % 2]
                    YW(s2, stg_t[ch % 2])
                    if ch == 0:
                        YW(ss, ss.c)
                        YW(sv, sv.c)
                    YE(lambda e, ch=ch, stg=stg: e.dma_start(out=stg, in_=xq8_sp.ap()[ch]), sy, 16)
                    TW(sy, sy.c)
                    if ch == 0:
                        TW(ss, ss.c)
                        TW(sv, sv.c)
                    for ts in range(NSP):
                        TE(lambda e, ts=ts, ch=ch, stg=stg, wT=wT, hp=hp: e.matmul(
                            pmm[:, ts, :], wT[:, ch * 512 + hp * 128: ch * 512 + hp * 128 + 128],
                            stg[:, ts * 512:(ts + 1) * 512],
                            start=(ch == 0), stop=(ch == NCH - 1)),
                            s2 if ts == NSP - 1 else None, 1)
                    stg_t[ch % 2] = s2.c
                proj_t = s2.c
                for ts in range(NSP):
                    SW(s2, proj_t)
                    if ts > 0:
                        SW(sv, sv.c)
                    SE(lambda e, ts=ts, acol=acol, bcol=bcol: e.activation(
                        tmp1[:, :], pmm[:, ts, :], AF.Identity,
                        bias=brow[:, bcol:bcol + 1], scale=sbc[:, acol:acol + 1]), ss, 1)
                    VW(ss, ss.c)
                    if ts == 0 and proj == 0:
                        VW(sg, sg.c)
                        VW(ss, pair_sq_ss)
                        VW(sy, sy.c)
                    VE(lambda e, ts=ts, dst_hi=dst_hi: e.tensor_copy(
                        dst_hi[:, ts * 512:(ts + 1) * 512], tmp1[:, :]), sv, 1)
                    VE(lambda e, ts=ts, dst_hi=dst_hi, dst_lo=dst_lo: e.scalar_tensor_tensor(
                        dst_lo[:, ts * 512:(ts + 1) * 512], tmp1[:, :], 1.0,
                        dst_hi[:, ts * 512:(ts + 1) * 512], OP.mult, OP.subtract), sv, 1)
            epi_v = sv.c

            SW(sv, epi_v)
            SE(lambda e: e.activation(sqq, pairQH, AF.Square, bias=0.0, scale=1.0), ss, 1)
            sq_q = ss.c
            for src, outc in ((0, 0), (1, 1)):
                if src == 1:
                    SW(s2, s2.c)
                    SE(lambda e: e.activation(sqq, pairKH, AF.Square, bias=0.0, scale=1.0), ss, 1)
                    pair_sq_ss = ss.c
                for ts in range(NSP):
                    TW(ss, ss.c)
                    if ts > 0 or src == 1:
                        TW(sv, sv.c)
                    TE(lambda e, ts=ts: e.matmul(
                        tpB[0:2, 0:512], selAB[:, 0:2], sqq[:, ts * 512:(ts + 1) * 512],
                        start=True, stop=True), s2, 1)
                    VW(s2, s2.c)
                    VE(lambda e: e.tensor_reduce(stats[0:2, 0:1], tpB[0:2, 0:512], AX, OP.max), sv, 1, dup=True)
                    VW(sv, sv.c)
                    if ts == 0:
                        VE(lambda e, outc=outc: e.tensor_copy(qn2[0:2, outc:outc + 1],
                                                              stats[0:2, 0:1]), sv, 1, dup=True)
                        VW(sv, sv.c)
                    else:
                        VE(lambda e, outc=outc: e.tensor_tensor(
                            qn2[0:2, outc:outc + 1], qn2[0:2, outc:outc + 1],
                            stats[0:2, 0:1], OP.max), sv, 1, dup=True)
                        VW(sv, sv.c)
            VE(lambda e: e.tensor_tensor(qn2[0:2, 2:3], qn2[0:2, 0:1], qn2[0:2, 1:2], OP.mult), sv, 1, dup=True)
            VW(sv, sv.c)
            m_v = sv.c
            SW(sv, m_v)
            SE(lambda e: e.sqrt(qn2[0:2, 3:4], qn2[0:2, 2:3]), ss, 1, dup=True)
            VW(ss, ss.c)
            VE(lambda e: e.tensor_scalar(qn2[0:2, 3:4], qn2[0:2, 3:4], -0.1275, None, OP.mult), sv, 1, dup=True)
            VW(sv, sv.c)
            mn_v = sv.c
            TW(sv, mn_v)
            TE(lambda e: e.matmul(tpB[0:1, 0:2], qn2[0:2, 3:4], ident32[0:2, 0:2],
                                  start=True, stop=True, is_transpose=True), s2, 1)
            VW(s2, s2.c)
            VE(lambda e: e.tensor_copy(mrow[0:1, 0:2], tpB[0:1, 0:2]), sv, 1, dup=True)
            VW(sv, sv.c)
            mr_v = sv.c
            TW(sv, mr_v)
            TE(lambda e: e.matmul(tpB[:, 8:10], onesr[0:1, :], mrow[0:1, 0:2],
                                  start=True, stop=True), s2, 1)
            VW(s2, s2.c)
            VE(lambda e, hp=hp: e.tensor_copy(mbc[:, 2 * hp:2 * hp + 2], tpB[:, 8:10]), sv, 1, dup=True)
            VW(sv, sv.c)

            stacks_g_h = [None, None]
            GW(sv, epi_v)
            GW(s2, stacks_free_t)
            for h01 in range(2):
                r0_ = 64 * h01
                stK, stQH, stQL = stKs[h01], stQHs[h01], stQLs[h01]
                GE(lambda e, r0_=r0_, stK=stK: e.dma_start(out=stK[0:64, :], in_=pairKH[r0_:r0_ + 64, :]), sg, 16)
                GE(lambda e, r0_=r0_, stK=stK: e.dma_start(out=stK[64:128, :], in_=pairKL[r0_:r0_ + 64, :]), sg, 16)
                GE(lambda e, r0_=r0_, stQH=stQH: e.dma_start(out=stQH[0:64, :], in_=pairQH[r0_:r0_ + 64, :]), sg, 16)
                GE(lambda e, r0_=r0_, stQH=stQH: e.dma_start(out=stQH[64:128, :], in_=pairQH[r0_:r0_ + 64, :]), sg, 16)
                GE(lambda e, r0_=r0_, stQL=stQL: e.dma_start(out=stQL[0:64, :], in_=pairQL[r0_:r0_ + 64, :]), sg, 16)
                GE(lambda e, r0_=r0_, stQL=stQL: e.dma_start(out=stQL[64:128, :], in_=pairQL[r0_:r0_ + 64, :]), sg, 16)
                stacks_g_h[h01] = sg.c
            YW(s2, va_free[0])
            YE(lambda e, hp=hp: e.dma_start(out=vaugs[0], in_=vaug_sp.ap()[2 * hp]), sy, 16)
            va_y[0] = sy.c
            YW(sg, stacks_g_h[1])
            YW(ss, pair_sq_ss)
            YW(s2, va_free[1])
            YE(lambda e, hp=hp: e.dma_start(out=vaugs[1], in_=vaug_sp.ap()[2 * hp + 1]), sy, 16)
            va_y[1] = sy.c

            for h01 in range(2):
                hloc = 2 * hp + h01
                stK, stQH, stQL = stKs[h01], stQHs[h01], stQLs[h01]
                vaug = vaugs[h01]
                stacks_g = stacks_g_h[h01]
                vaug_y = va_y[h01]

                exp_done = {}
                sub_done = {}
                pv_done = {}
                comb_done = sv.c
                sc_free = [ss.c, ss.c, ss.c]
                slot_pv = [s2.c] * RINGN
                pf_sub = [sv.c, sv.c]
                TW(sg, stacks_g)
                TW(sy, vaug_y)
                pva = [pmm[:, 6 + sb // 2, (sb % 2) * 256:(sb % 2) * 256 + 129] for sb in range(4)]
                denb = lnst[:, 0:NT]
                denr = lnst[:, NT:2 * NT]
                started_bank = {}
                qk_t = {}
                comb_of_stripe = {}

                def emit_qk2(j, kt, pp):
                    c = max(0, kt - 4 * j)
                    scb = pmm[:, 2 * pp + (kt % 2), :]
                    diag = (kt >= 4 * j)
                    TW(ss, sc_free[pp])
                    TE(lambda e, scb=scb, kt=kt, j=j: e.matmul(
                        scb[0:128, 0:512], stK[:, kt * 128:(kt + 1) * 128],
                        stQH[:, j * 512:(j + 1) * 512],
                        start=True, stop=False), None)
                    TE(lambda e, scb=scb, kt=kt, j=j, diag=diag: e.matmul(
                        scb[0:128, 0:512], stK[:, kt * 128:(kt + 1) * 128],
                        stQL[:, j * 512:(j + 1) * 512],
                        start=False, stop=(not diag)),
                        s2 if not diag else None, 1)
                    if diag:
                        TE(lambda e, scb=scb, c=c: e.matmul(
                            scb[0:128, 0:512], Av, Bv[c][:, 0:512],
                            start=False, stop=True), s2, 1)
                    qk_t[(j, kt)] = s2.c

                def emit_tail2(j, bb, pp, p2, gb):
                    kt0, kt1 = 2 * bb, 2 * bb + 1
                    scpair = pmm[:, 2 * pp:2 * pp + 2, :]
                    slot = (2 * gb) % RINGN
                    SW(s2, qk_t[(j, kt1)])
                    SW(s2, slot_pv[slot])
                    SW(s2, slot_pv[slot + 1])
                    SW(sv, pf_sub[p2])
                    phpair = big2[:, r0 + slot * 512: r0 + (slot + 2) * 512]
                    plpair = big2[:, r1 + slot * 512: r1 + (slot + 2) * 512]
                    SE(lambda e, scpair=scpair, phpair=phpair, hloc=hloc: e.activation(
                        phpair, scpair, AF.Exp,
                        bias=mbc[:, hloc:hloc + 1], scale=cpool[:, CP_125:CP_125 + 1]), ss, 1)
                    SE(lambda e, scpair=scpair, p2=p2, hloc=hloc: e.activation(
                        pf32[p2][:, :], scpair, AF.Exp,
                        bias=mbc[:, hloc:hloc + 1], scale=cpool[:, CP_125:CP_125 + 1]), ss, 1)
                    exp_done[(j, bb)] = ss.c
                    sc_free[pp] = ss.c
                    VW(ss, exp_done[(j, bb)])
                    VW(s2, slot_pv[slot])
                    VW(s2, slot_pv[slot + 1])
                    VE(lambda e, p2=p2, phpair=phpair, plpair=plpair: e.scalar_tensor_tensor(
                        plpair, pf32[p2][:, :], 1.0, phpair, OP.mult, OP.subtract), sv, 1)
                    sub_done[(j, bb)] = sv.c
                    pf_sub[p2] = sv.c
                    TW(sv, sub_done[(j, bb)])
                    if bb == 0:
                        TW(sv, comb_of_stripe.get(j - 1, comb_done))
                    for kt in (kt0, kt1):
                        c = max(0, kt - 4 * j)
                        slot_k = (2 * gb + (kt % 2)) % RINGN
                        for sb in range(c, 4):
                            bk_i = sb // 2
                            first = not started_bank.get((j, bk_i), False)
                            started_bank[(j, bk_i)] = True
                            stop_b = (sb % 2 == 1) and (kt == 4 * j + sb)
                            TE(lambda e, sb=sb, slot_k=slot_k, kt=kt, first=first: e.matmul(
                                pva[sb], ringPh[slot_k][:, sb * 128:(sb + 1) * 128],
                                vaug[:, kt * 129:(kt + 1) * 129],
                                start=first, stop=False), None)
                            TE(lambda e, sb=sb, slot_k=slot_k, kt=kt, stop_b=stop_b: e.matmul(
                                pva[sb][:, 0:65], ringPl[slot_k][:, sb * 128:(sb + 1) * 128],
                                vaug[:, kt * 129:kt * 129 + 65],
                                start=False, stop=stop_b),
                                s2 if sb == 3 else None, 1)
                        pv_done[(j, kt)] = s2.c
                        slot_pv[slot_k] = s2.c

                def emit_combines(j):
                    nonlocal comb_done
                    for sb in range(4):
                        qt = 4 * j + sb
                        VW(s2, pv_done[(j, 4 * j + (sb | 1))])
                        if j == 0 and sb == 0:
                            VW(sy, yhead_free_y)
                        VE(lambda e, sb=sb: e.tensor_copy(
                            tmp2[:, 64:128], pva[sb][:, 65:129]), sv, 1)
                        VE(lambda e, sb=sb, qt=qt: e.tensor_add(
                            yhead[:, qt * 64:(qt + 1) * 64], tmp2[:, 64:128],
                            pva[sb][:, 0:64]), sv, 1)
                        VE(lambda e, sb=sb, qt=qt: e.tensor_copy(
                            denb[:, qt:qt + 1], pva[sb][:, 64:65]), sv, 1, dup=True)
                    comb_done = sv.c
                    comb_of_stripe[j] = sv.c

                seq = [(j, bb) for j in range(NT // 4) for bb in range(2 * j + 2)]
                for i, (j, bb) in enumerate(seq):
                    pp = i % 3
                    p2 = i % 2
                    emit_qk2(j, 2 * bb, pp)
                    emit_qk2(j, 2 * bb + 1, pp)
                    if i >= 2:
                        j2, bb2 = seq[i - 2]
                        emit_tail2(j2, bb2, (i - 2) % 3, (i - 2) % 2, i - 2)
                        if bb2 == 2 * j2 + 1:
                            emit_combines(j2)
                for i in (len(seq) - 2, len(seq) - 1):
                    j2, bb2 = seq[i]
                    emit_tail2(j2, bb2, i % 3, i % 2, i)
                    if bb2 == 2 * j2 + 1:
                        emit_combines(j2)

                VW(sv, comb_done)
                VE(lambda e: e.reciprocal(denr, denb), sv, 1, dup=True)
                VW(sv, sv.c)
                for qt in range(NT):
                    VE(lambda e, qt=qt: e.tensor_scalar(
                        yhead[:, qt * 64:(qt + 1) * 64], yhead[:, qt * 64:(qt + 1) * 64],
                        denr[:, qt:qt + 1], None, OP.mult), sv, 1)
                VE(lambda e: e.tensor_reduce(stats[:, 1:2], yhead[:, 0:NT * 64], AX, OP.max,
                                             apply_absolute_value=True), sv, 1, dup=True)
                VW(sv, sv.c)
                if hloc == 0:
                    VE(lambda e: e.tensor_copy(scal[:, MC_Y:MC_Y + 1], stats[:, 1:2]), sv, 1, dup=True)
                else:
                    VE(lambda e: e.tensor_tensor(
                        scal[:, MC_Y:MC_Y + 1], scal[:, MC_Y:MC_Y + 1],
                        stats[:, 1:2], OP.max), sv, 1, dup=True)
                VW(sv, sv.c)
                YW(sv, sv.c)
                YE(lambda e, hloc=hloc: e.dma_start(out=y_sp.ap()[hloc], in_=yhead), sy, 16)
                yhead_free_y = sy.c
                stacks_free_t = s2.c
                va_free[h01] = s2.c

    def phaseC(cur, eng):
        mk = mkwalk(cur, eng)
        VE, VW = mk("vector")
        TE, TW = mk("tensor")
        SE, SW = mk("scalar")
        YE, YW = mk("sync")
        GE, GW = mk("gpsimd")

        VW(sy, sy.c)
        VE(lambda e: e.tensor_copy(tmp1[:, 0:1], scal[:, MC_Y:MC_Y + 1]), sv, 1)
        ym_v = sv.c
        TW(sv, ym_v)
        TE(lambda e: e.matmul(tp32[0:1, 0:128], tmp1[:, 0:1], ident32[:, :],
                              start=True, stop=True, is_transpose=True), s2, 1)
        VW(s2, s2.c)
        VE(lambda e: e.tensor_reduce(stats[0:1, 0:1], tp32[0:1, 0:128], AX, OP.max), sv, 1, dup=True)
        VW(sv, sv.c)
        red_v = sv.c
        GW(sv, red_v)
        GE(lambda e: e.dma_start(out=cc2_in.ap()[0:1], in_=stats[0:1, 0:1]), sg, 16)
        GW(sg, sg.c)
        GE(lambda e: e.collective_compute(
            "AllReduce", OP.max, replica_groups=[list(range(8))],
            ins=[cc2_in.ap().opt()], outs=[cc2_out.ap().opt()]), scc, 1)
        GW(scc, scc.c)
        GE(lambda e: e.dma_start(out=srow[0:1, 14:15], in_=cc2_out.ap()[0:1]), sg, 16)
        cc2_g = sg.c

        if CSTEP <= 1:
            return
        for ch in range(4):
            GE(lambda e, ch=ch: e.dma_start(out=wfraw[ch], in_=wf_d[ch * 128:(ch + 1) * 128, :]), sg, 16)
        wf_g = sg.c

        YE(lambda e: e.dma_start(out=tmp2[0:1, :], in_=bo_d), sy, 16)
        bo_y = sy.c
        if CSTEP <= 11:
            return

        VW(sg, cc2_g)
        VE(lambda e: e.tensor_scalar(srow[0:1, 15:16], srow[0:1, 14:15], 1.0 / 127.0, 1e-8,
                                     OP.mult, OP.max), sv, 1, dup=True)
        VW(sv, sv.c)
        VE(lambda e: e.reciprocal(mrow[0:1, 8:9], srow[0:1, 15:16]), sv, 1, dup=True)
        VW(sv, sv.c)
        VE(lambda e: e.tensor_single_scalar(
            mrow[0:1, 9:10], srow[0:1, 15:16], srow[0:1, 12:13], OP.mult), sv, 1, dup=True)
        VW(sv, sv.c)
        ry_v = sv.c
        TW(sv, ry_v)
        TE(lambda e: e.matmul(tp32[:, 0:2], onesr[0:1, :], mrow[0:1, 8:10],
                              start=True, stop=True), s2, 1)
        VW(s2, s2.c)
        VE(lambda e: e.tensor_copy(sbc[:, SC_RY:SC_RY + 2], tp32[:, 0:2]), sv, 1, dup=True)
        VW(sv, sv.c)
        ry_bc = sv.c
        if CSTEP <= 12:
            return

        TW(sy, bo_y)
        TW(sv, ry_bc)
        TE(lambda e: e.matmul(tp32[:, 0:512], onesr[0:1, :], tmp2[0:1, :],
                              start=True, stop=True), s2, 1)
        VW(s2, s2.c)
        VE(lambda e: e.tensor_copy(borep[:, :], tp32[:, 0:512]), sv, 1)

        if CSTEP <= 2:
            return
        VW(sg, wf_g)
        for ch in range(4):
            raw = wfraw[ch]
            VE(lambda e, raw=raw: e.tensor_reduce(stats[:, 0:1], raw, AX, OP.add,
                                                  apply_absolute_value=True), sv, 1, dup=True)
            VW(sv, sv.c)
            VE(lambda e, ch=ch: e.tensor_scalar(
                wfs[:, ch:ch + 1], stats[:, 0:1], 1.0 / 1024.0, 1e-5, OP.mult, OP.max), sv, 1, dup=True)
            VW(sv, sv.c)
            VE(lambda e, ch=ch: e.reciprocal(stats[:, 1:2], wfs[:, ch:ch + 1]), sv, 1, dup=True)
            VW(sv, sv.c)
            VE(lambda e, raw=raw: e.tensor_scalar(
                xnt[:, 0:1024], raw, stats[:, 1:2], 1.0, OP.mult, OP.min), sv, 1)
            VE(lambda e: e.tensor_scalar(
                q16[:, :], xnt[:, 0:1024], -1.0, 1536.0, OP.max, OP.add), sv, 1)
            if ch > 0:
                VW(s2, s2.c)
            VE(lambda e: e.tensor_scalar(qtm[:, :], q16[:, :], -1536.0, None, OP.add), sv, 1)
            qv = sv.c
            TW(sv, qv)
            for c2 in range(8):
                TE(lambda e, c2=c2: e.matmul(
                    tp0[:, c2 * 128:(c2 + 1) * 128], qtm[:, c2 * 128:(c2 + 1) * 128],
                    identv, start=(c2 == 0), stop=(c2 == 7), is_transpose=True),
                    s2 if c2 == 7 else None, 1)
            tv = s2.c
            VW(s2, tv)
            dst = bass.AP(wt, wfT.offset + ch * 128,
                          [[wt.shape[1], 128], [512, 8], [1, 128]])
            VE(lambda e, dst=dst: e.tensor_copy(dst, tp0.rearrange("p (a b) -> p a b", b=128)), sv, 1)
        wf_done_v = sv.c

        TW(sv, wf_done_v)
        TE(lambda e: e.matmul(tp32[0:4, 0:128], wfs[:, 0:4], ident32[:, :],
                              start=True, stop=True, is_transpose=True), s2, 1)
        VW(s2, s2.c)
        VE(lambda e: e.tensor_copy(tmp1[0:4, 0:128], tp32[0:4, 0:128]), sv, 1)
        ws_v = sv.c
        GW(sv, ws_v)
        GE(lambda e: e.dma_start(out=wsb.ap(), in_=tmp1[0:4, 0:128]), sg, 16)
        GW(sg, sg.c)
        GE(lambda e: e.dma_start(out=tmp2[0:1, :], in_=wsb.ap()), sg, 16)
        wsrow_g = sg.c
        TW(sg, wsrow_g)
        TE(lambda e: e.matmul(tp32[:, 0:512], onesr[0:1, :], tmp2[0:1, :],
                              start=True, stop=True), s2, 1)
        VW(s2, s2.c)
        VE(lambda e: e.tensor_copy(wsrep[:, :], tp32[:, 0:512]), sv, 1)

        if CSTEP <= 3:
            return
        VW(sv, ry_bc)
        nhalf = NT // 2
        for hloc in range(NHC):
            YW(sv, sv.c)
            YE(lambda e, hloc=hloc: e.dma_start(out=ystg[:, 0:NT * HD], in_=y_sp.ap()[hloc]), sy, 16)
            VW(sy, sy.c)
            for half in range(2):
                VE(lambda e, half=half: e.tensor_scalar(
                    q16[:, 0:nhalf * 64], ystg[:, half * nhalf * 64:(half + 1) * nhalf * 64],
                    sbc[:, SC_RY:SC_RY + 1], cpool[:, CP_1536:CP_1536 + 1],
                    OP.mult, OP.add), sv, 1)
                dstq = bass.AP(big2, half * nhalf * 512 + hloc * 64,
                               [[32768, 128], [512, nhalf], [1, 64]])
                srcq = q16[:, 0:nhalf * 64].rearrange("p (a b) -> p a b", b=64)
                VE(lambda e, dstq=dstq, srcq=srcq: e.tensor_scalar(
                    dstq, srcq, -1536.0, None, OP.add), sv, 1)
        yq_v = sv.c

        if CSTEP <= 4:
            return
        GW(sv, yq_v)
        GE(lambda e: e.dma_start(
            out=ag_in.ap().rearrange("(j p) i -> p j i", p=128),
            in_=big2[:, 0:NT * 512].rearrange("p (j i) -> p j i", i=512)), sg, 16)
        GW(sg, sg.c)
        GE(lambda e: e.collective_compute(
            "AllGather", OP.bypass, replica_groups=[[0, 1], [2, 3], [4, 5], [6, 7]],
            ins=[ag_in.ap().opt()], outs=[ag_out.ap().opt()]), scc, 1)
        ag_cc = scc.c

        if CSTEP <= 5:
            return
        stg_y = [sv.c, sv.c]
        YW(scc, ag_cc)
        for r in range(2):
            for jt in range(NT):
                k = r * NT + jt
                stg = aq_stg[k % 2]
                YW(sv, stg_y[k % 2])
                YE(lambda e, r=r, jt=jt, stg=stg: e.dma_start(
                    out=stg, in_=ag_out.ap()[r, jt * 128:(jt + 1) * 128, :]), sy, 16)
                TW(sy, sy.c)
                if k >= 1:
                    TW(sv, sv.c)
                for ic in range(4):
                    TE(lambda e, ic=ic, stg=stg: e.matmul(
                        tp0[:, ic * 128:(ic + 1) * 128], stg[:, ic * 128:(ic + 1) * 128],
                        identv, start=(ic == 0), stop=(ic == 3), is_transpose=True),
                        s2 if ic == 3 else None, 1)
                VW(s2, s2.c)
                dsty = bass.AP(big1, (r * 4) * T + jt * 128,
                               [[32832, 128], [T, 4], [1, 128]])
                VE(lambda e, dsty=dsty: e.tensor_copy(
                    dsty, tp0[:, 0:512].rearrange("p (a b) -> p a b", b=128)), sv, 1)
                stg_y[k % 2] = sv.c
        yqf_v = sv.c

        if CSTEP <= 6:
            return
        ld_q, ld_y, scrd, mmt, obv = {}, {}, {}, {}, {}
        bC_sv, bC_s2, bC_sq, bC_sy = sv.c, s2.c, sq.c, sy.c

        def emit_loads(jt):
            par = jt % 2
            SW(s2, mmt.get(jt - 2, bC_s2))
            SW(sv, bC_sv if jt < 2 else 0)
            SE(lambda e, jt=jt, par=par: e.dma_start(
                out=xq4stg[par].rearrange("p (c t) -> p c t", t=128),
                in_=bass.AP(xq4_sp, jt * 128, [[T, 128], [128 * T, 8], [1, 128]])), sq, 16)
            ld_q[jt] = sq.c
            YW(sv, obv.get(jt - 2, bC_sv))
            YE(lambda e, jt=jt, par=par: e.dma_start(
                out=xot[par], in_=x_ocol[jt * 128:(jt + 1) * 128, :]), sy, 16)
            ld_y[jt] = sy.c

        emit_loads(0)
        emit_loads(1)
        TW(sv, yqf_v)
        for jt in range(NT):
            par = jt % 2
            TW(sv, scrd.get(jt - 2, bC_sv))
            for ch in range(NCH):
                TE(lambda e, ch=ch, jt=jt, par=par: e.matmul(
                    mm_b[par], yqf[ch][:, jt * 128:(jt + 1) * 128],
                    woT[:, ch * 512:(ch + 1) * 512],
                    start=(ch == 0), stop=(ch == NCH - 1)),
                    s2 if ch == NCH - 1 else None, 1)
            TW(sq, ld_q[jt])
            for ch in range(NCH):
                TE(lambda e, ch=ch, par=par: e.matmul(
                    sc_b[par], xq4stg[par][:, ch * 128:(ch + 1) * 128],
                    wfT[:, ch * 512:(ch + 1) * 512],
                    start=(ch == 0), stop=(ch == NCH - 1)),
                    s2 if ch == NCH - 1 else None, 1)
            mmt[jt] = s2.c
            VW(s2, mmt[jt])
            VW(sy, ld_y[jt])
            VE(lambda e, par=par: e.scalar_tensor_tensor(
                tmp1[:, :], mm_b[par], sbc[:, SC_AO:SC_AO + 1], borep[:, :],
                OP.mult, OP.add), sv, 1)
            VE(lambda e, par=par, jt=jt: e.scalar_tensor_tensor(
                tmp2[:, :], sc_b[par], rs4[:, jt:jt + 1], wsrep[:, :],
                OP.mult, OP.mult), sv, 1)
            scrd[jt] = sv.c
            VE(lambda e: e.tensor_add(tmp1[:, :], tmp1[:, :], tmp2[:, :]), sv, 1)
            if jt >= 2:
                VW(sd, 16 * (jt - 1))
            VE(lambda e, par=par: e.tensor_add(obuf[par], tmp1[:, :], xot[par]), sv, 1)
            obv[jt] = sv.c
            if jt + 2 < NT:
                emit_loads(jt + 2)
            YW(sv, obv[jt])
            YE(lambda e, jt=jt, par=par: e.dma_start(
                out=out_d[jt * 128:(jt + 1) * 128, :], in_=obuf[par]), sd, 16)
        YW(sd, sd.c)

        if debug:
            GW(sd, sd.c)
            for nm, src in (("xn", xn_sp), ("xq8", xq8_sp), ("vaug", vaug_sp),
                            ("y", y_sp), ("xq4", xq4_sp), ("yq", ag_in)):
                if nm in dbg:
                    GE(lambda e, nm=nm, src=src: e.dma_start(
                        out=dbg[nm], in_=src.ap()), sg, 16)
            if "sca" in dbg:
                GE(lambda e: e.dma_start(out=dbg["sca"], in_=srow[0:1, :]), sg, 16)
            if "yqf" in dbg:
                GE(lambda e: e.dma_start(out=dbg["yqf"], in_=big1[:, 0:NCH * T]), sg, 16)
            if "woT" in dbg:
                GE(lambda e: e.dma_start(out=dbg["woT"], in_=woT), sg, 16)
            if "wfT" in dbg:
                GE(lambda e: e.dma_start(out=dbg["wfT"], in_=wfT), sg, 16)
            if "wsrep" in dbg:
                GE(lambda e: e.dma_start(out=dbg["wsrep"], in_=wsrep), sg, 16)
            if "rs4" in dbg:
                GE(lambda e: e.dma_start(out=dbg["rs4"], in_=rs4[:, 0:NT]), sg, 16)
            GW(sg, sg.c)

    phases = [phaseA, phaseB, phaseC][:PHASES]
    for ph in phases:
        with nc.Block() as blk:
            done = {}

            def reg(engname, attr):
                def closure(e):
                    ph(engname, e)
                    for s in sems:
                        done[id(s)] = s.c
                return closure
            blk.sync(reg("sync", None))
            blk.vector(reg("vector", None))
            blk.scalar(reg("scalar", None))
            blk.tensor(reg("tensor", None))
            blk.gpsimd(reg("gpsimd", None))
            for s in sems:
                snap[id(s)] = done[id(s)]

    stx.close()
    return nc


_NC_CACHE = {}


def _get_nc(debug=False):
    if debug not in _NC_CACHE:
        _NC_CACHE[debug] = build_nc(debug)
    return _NC_CACHE[debug]


def kernel(**inputs):
    X = np.ascontiguousarray(np.asarray(inputs["L0_attn_x"], dtype=np.float32))
    Wq = np.asarray(inputs["L0_attn_Wq"], dtype=np.float32)
    Wk = np.asarray(inputs["L0_attn_Wk"], dtype=np.float32)
    Wv = np.asarray(inputs["L0_attn_Wv"], dtype=np.float32)
    Wo = np.asarray(inputs["L0_attn_Wo"], dtype=np.float32)
    Wf = np.asarray(inputs["L0_ffn_weight"], dtype=np.float32)
    bq = np.asarray(inputs["L0_attn_bq"], dtype=np.float32)
    bk = np.asarray(inputs["L0_attn_bk"], dtype=np.float32)
    bv = np.asarray(inputs["L0_attn_bv"], dtype=np.float32)
    bo = np.asarray(inputs["L0_attn_bo"], dtype=np.float32)

    nc = _get_nc(False)
    in_maps = []
    for c in range(8):
        b, g = c // 2, c % 2
        sl = slice(g * 512, (g + 1) * 512)
        in_maps.append({
            "xb": X[b],
            "x_ocol": np.ascontiguousarray(X[b][:, sl]),
            "wq": np.ascontiguousarray(Wq[sl]),
            "wk": np.ascontiguousarray(Wk[sl]),
            "wv": np.ascontiguousarray(Wv[sl]),
            "wo": np.ascontiguousarray(Wo[sl]),
            "wf": np.ascontiguousarray(Wf[sl]),
            "bq": np.ascontiguousarray(bq[sl]),
            "bk": np.ascontiguousarray(bk[sl]),
            "bv": np.ascontiguousarray(bv[sl]),
            "bo": np.ascontiguousarray(bo[sl]),
        })
    res = run_bass_kernel_spmd(nc, in_maps, core_ids=list(range(8)))
    out = np.empty((B_, T, D), np.float32)
    for c in range(8):
        b, g = c // 2, c % 2
        out[b][:, g * 512:(g + 1) * 512] = res.results[c]["out"]
    return out
